# version 07
# baseline (speedup 1.0000x reference)
import numpy as np
import ml_dtypes
from contextlib import ExitStack

import concourse.bass as bass
import concourse.mybir as mybir
from concourse.bass_utils import run_bass_kernel_spmd

F32 = mybir.dt.float32
BF16 = mybir.dt.bfloat16
AF = mybir.ActivationFunctionType
ALU = mybir.AluOpType

ENGINES = ("sync", "scalar", "vector", "tensor", "gpsimd")

D = 1024
L = 4096
N_ST = 16
DC = 8
TB = 1024
NTB = L // TB
EPS = 1e-5



class Chan:

    def __init__(self, handle, dma=False):
        self.h = handle
        self.dma = dma
        self.count = 0
        self.amt = 16 if dma else 1


class Buf:

    def __init__(self, ap, name=""):
        self.ap = ap
        self.name = name
        self.last_write = None
        self.readers = []


class Prog:
    def __init__(self, nc):
        self.nc = nc
        self.steps = {e: [] for e in ENGINES}
        self.hwm = {e: {} for e in ENGINES}
        self.all_chans = []

    def chan(self, es, name, dma=False):
        h = es.enter_context(self.nc.semaphore(name))
        c = Chan(h, dma=dma)
        self.all_chans.append(c)
        return c

    def emit(self, engine, fn, reads=(), writes=(), extra_waits=(), inc=None,
             war=()):
        need = {}

        def add(cc):
            if cc is None:
                return
            ch, v = cc
            if v <= 0:
                return
            if need.get(ch, 0) < v:
                need[ch] = v

        for b in reads:
            add(b.last_write)
        for b in list(writes) + list(war):
            add(b.last_write)
            for r in b.readers:
                add(r)
        for w in extra_waits:
            add(w if isinstance(w, tuple) else (w, w.count))

        if inc is not None and inc.dma:
            add((inc, inc.count))
        hw = self.hwm[engine]
        waits = []
        for ch, v in need.items():
            if hw.get(ch, 0) < v:
                waits.append((ch, v))
                hw[ch] = v
        if inc is not None:
            inc.count += inc.amt
        self.steps[engine].append((fn, waits, inc))
        tag = (inc, inc.count) if inc is not None else None
        for b in writes:
            b.last_write = tag
            b.readers = []
        if tag is not None:
            for b in reads:
                b.readers.append(tag)

    def barrier(self):
        for e in ENGINES:
            for ch in self.all_chans:
                if ch.count > 0 and self.hwm[e].get(ch, 0) < ch.count:
                    self.steps[e].append((None, [(ch, ch.count)], None))
                    self.hwm[e][ch] = ch.count

    def build(self, block):
        def mk(engine):
            def run(eng):
                for fn, waits, inc in self.steps[engine]:
                    for ch, v in waits:
                        eng.wait_ge(ch.h, v)
                    if fn is not None:
                        ins = fn(eng)
                        if inc is not None:
                            assert ins is not None
                            ins.then_inc(inc.h, inc.amt)
                    elif inc is not None:
                        eng.sem_inc(inc.h, inc.amt)
            return run
        block.sync(mk("sync"))
        block.scalar(mk("scalar"))
        block.vector(mk("vector"))
        block.tensor(mk("tensor"))
        block.gpsimd(mk("gpsimd"))



def build_program():
    nc = bass.Bass("TRN2", target_bir_lowering=False)

    xT = nc.dram_tensor("xT", [D, L], F32, kind="ExternalInput").ap()
    A_log_r = nc.dram_tensor("A_log_r", [128, DC * N_ST], F32, kind="ExternalInput").ap()
    Dp_c = nc.dram_tensor("Dp_c", [128, DC], F32, kind="ExternalInput").ap()
    Wxp = nc.dram_tensor("Wxp", [D, D + 2 * N_ST], F32, kind="ExternalInput").ap()
    bxp_c = nc.dram_tensor("bxp_c", [128, DC], F32, kind="ExternalInput").ap()
    bbc_c = nc.dram_tensor("bbc_c", [2 * N_ST, 1], F32, kind="ExternalInput").ap()
    Wout = nc.dram_tensor("Wout", [D, D], F32, kind="ExternalInput").ap()
    bout_c = nc.dram_tensor("bout_c", [128, DC], F32, kind="ExternalInput").ap()
    theta = nc.dram_tensor("theta", [D, D], F32, kind="ExternalInput").ap()
    phi = nc.dram_tensor("phi", [D, D], F32, kind="ExternalInput").ap()
    theta_n = nc.dram_tensor("theta_n", [D, D], F32, kind="ExternalInput").ap()
    phi_n = nc.dram_tensor("phi_n", [D, D], F32, kind="ExternalInput").ap()
    c_col = nc.dram_tensor("c_col", [128, DC], F32, kind="ExternalInput").ap()
    wmod = nc.dram_tensor("wmod", [D, 6 * D], F32, kind="ExternalInput").ap()

    outT = nc.dram_tensor("outT", [D, L], F32, kind="ExternalOutput").ap()

    h1_scr = nc.dram_tensor("h1_scr", [DC, 128, L], BF16).ap()
    du_scr = nc.dram_tensor("du_scr", [DC, 128, 2 * L], BF16).ap()
    wo_scr = nc.dram_tensor("wo_scr", [D, D], BF16).ap()
    wc_scr = nc.dram_tensor("wc_scr", [D, D], BF16).ap()
    ws_scr = nc.dram_tensor("ws_scr", [D, D], BF16).ap()

    es = ExitStack()
    with es:
        p = Prog(nc)

        def sb(name, shape, dt):
            return es.enter_context(nc.sbuf_tensor(name, shape, dt))

        big1 = sb("big1", [128, 8 * L], BF16)
        big2 = sb("big2", [128, 8 * L], BF16)
        big4 = sb("big4", [128, 3 * L], BF16)
        wor = sb("wor", [128, 2 * DC * 128], BF16)
        wcr = sb("wcr", [128, 2 * DC * 128], BF16)
        wsr = sb("wsr", [128, 2 * DC * 128], BF16)
        bc_sb = sb("bc_sb", [2 * N_ST, L], BF16)
        bcstg = sb("bcstg", [1, L], BF16)
        cons = sb("cons", [128, 64], F32)
        a_sc = sb("a_sc", [128, 128], F32)
        mc_adj = sb("mc_adj", [128, 32], BF16)
        rows1 = sb("rows1", [2, D], BF16)
        rows2 = sb("rows2", [2, D], BF16)
        ab_a = sb("ab_a", [1, TB], BF16)
        ab_bo = sb("ab_bo", [2, TB], BF16)
        rowf = sb("rowf", [1, 3 * TB], F32)
        ones1 = sb("ones1", [128, 1], BF16)
        ones_r = sb("ones_r", [1, 128], BF16)
        sio = sb("sio", [128, DC], F32)
        siob = sb("siob", [128, DC], BF16)
        gcols = sb("gcols", [128, 4 * DC], F32)
        mcols_f = sb("mcols_f", [128, 48], F32)

        ps_all = es.enter_context(nc.psum_tensor("ps_all", [128, 4 * TB], F32))

        d_g = p.chan(es, "d_g", dma=True)
        d_s = p.chan(es, "d_s", dma=True)
        d_w = p.chan(es, "d_w", dma=True)
        d_o = p.chan(es, "d_o", dma=True)
        cv = p.chan(es, "cv")
        cs = p.chan(es, "cs")
        ct = p.chan(es, "ct")
        cg = p.chan(es, "cg")

        block = es.enter_context(nc.Block())

        asc_b = Buf(a_sc[:], "a_sc")
        p.emit("sync", lambda s: s.dma_start(out=a_sc[:], in_=A_log_r),
               writes=[asc_b], inc=d_s)
        p.emit("scalar", lambda s: s.activation(a_sc[:], a_sc[:], AF.Exp),
               reads=[asc_b], writes=[asc_b], inc=cs)
        p.emit("vector", lambda v: v.tensor_scalar_mul(a_sc[:], a_sc[:], -1.0),
               reads=[asc_b], writes=[asc_b], inc=cv)

        dp_b = Buf(cons[:, 0:DC], "dp")
        bxpc_b = Buf(cons[:, 8:8 + DC], "bxp")
        boutc_b = Buf(cons[:, 16:16 + DC], "bout")
        bbc_b = Buf(cons[0:2 * N_ST, 24:25], "bbc")
        p.emit("sync", lambda s: s.dma_start(out=cons[:, 0:DC], in_=Dp_c),
               writes=[dp_b], inc=d_s)
        p.emit("sync", lambda s: s.dma_start(out=cons[:, 8:8 + DC], in_=bxp_c),
               writes=[bxpc_b], inc=d_s)
        p.emit("sync", lambda s: s.dma_start(out=cons[:, 16:16 + DC], in_=bout_c),
               writes=[boutc_b], inc=d_s)
        p.emit("sync", lambda s: s.dma_start(out=cons[0:2 * N_ST, 24:25], in_=bbc_c),
               writes=[bbc_b], inc=d_s)

        sio_b = Buf(sio[:], "sio")
        siob_b = Buf(siob[:], "siob")
        p.emit("sync", lambda s: s.dma_start(out=sio[:], in_=c_col),
               writes=[sio_b], inc=d_s)
        p.emit("scalar", lambda s: s.activation(sio[:], sio[:], AF.Silu),
               reads=[sio_b], writes=[sio_b], inc=cs)
        p.emit("vector", lambda v: v.tensor_copy(siob[:], sio[:]),
               reads=[sio_b], writes=[siob_b], inc=cv)

        stg_a = Buf(big2[:, 0:8 * D], "stg_a")
        stg_b = Buf(big2[:, 8 * D:16 * D], "stg_b")
        stg_c = Buf(big2[:, 16 * D:24 * D], "stg_c")
        stg_d = Buf(big2[:, 24 * D:32 * D], "stg_d")

        def cast_weight_to_scr(w_in, w_out, stg):
            src = w_in.rearrange("(c p) j -> p c j", p=128)
            dst = w_out.rearrange("(c p) j -> p c j", p=128)
            sview = stg.ap.rearrange("p (c j) -> p c j", c=DC)
            p.emit("gpsimd", lambda g: g.dma_start(out=sview, in_=src),
                   writes=[stg], inc=d_g)
            p.emit("sync", lambda s: s.dma_start(out=dst, in_=sview),
                   reads=[stg], inc=d_w)

        cast_weight_to_scr(Wout, wo_scr, stg_a)

        def build_trig(w1, w2, w_out, stg1, stg2, is_cos):
            src1 = w1.rearrange("(c p) j -> p c j", p=128)
            src2 = w2.rearrange("(c p) j -> p c j", p=128)
            s1 = stg1.ap
            s2 = stg2.ap
            v1 = s1.rearrange("p (c j) -> p c j", c=DC)
            v2 = s2.rearrange("p (c j) -> p c j", c=DC)
            p.emit("gpsimd", lambda g: g.dma_start(out=v1, in_=src1),
                   writes=[stg1], inc=d_g)
            p.emit("gpsimd", lambda g: g.dma_start(out=v2, in_=src2),
                   writes=[stg2], inc=d_g)
            p.emit("vector", lambda v: v.tensor_add(s1, s1, s2),
                   reads=[stg1, stg2], writes=[stg1], inc=cv)
            p.emit("vector", lambda v: v.tensor_mul(s2, s1, s1),
                   reads=[stg1], writes=[stg2], inc=cv)
            if is_cos:
                p.emit("vector", lambda v: v.tensor_scalar(s2, s2, 1.0 / 24.0, -0.5,
                                                           ALU.mult, ALU.add),
                       reads=[stg2], writes=[stg2], inc=cv)
                p.emit("vector", lambda v: v.tensor_mul(s2, s2, s1),
                       reads=[stg1, stg2], writes=[stg2], inc=cv)
                p.emit("vector", lambda v: v.tensor_mul(s2, s2, s1),
                       reads=[stg1, stg2], writes=[stg2], inc=cv)
                p.emit("vector", lambda v: v.tensor_scalar_add(s1, s2, 1.0),
                       reads=[stg2], writes=[stg1], inc=cv)
            dst = w_out.rearrange("(c p) j -> p c j", p=128)
            sview = s1.rearrange("p (c j) -> p c j", c=DC)
            p.emit("sync", lambda s: s.dma_start(out=dst, in_=sview),
                   reads=[stg1], inc=d_w)

        build_trig(theta, theta_n, wc_scr, stg_a, stg_b, True)

        srcp = phi.rearrange("(c p) j -> p c j", p=128)
        srcpn = phi_n.rearrange("(c p) j -> p c j", p=128)
        sA, sB = stg_c, stg_d
        vA = sA.ap.rearrange("p (c j) -> p c j", c=DC)
        vB = sB.ap.rearrange("p (c j) -> p c j", c=DC)
        p.emit("gpsimd", lambda g: g.dma_start(out=vA, in_=srcp), writes=[sA], inc=d_g)
        p.emit("gpsimd", lambda g: g.dma_start(out=vB, in_=srcpn), writes=[sB], inc=d_g)
        p.emit("vector", lambda v: v.tensor_add(sA.ap, sA.ap, sB.ap),
               reads=[sA, sB], writes=[sA], inc=cv)
        p.emit("vector", lambda v: v.tensor_mul(sB.ap, sA.ap, sA.ap),
               reads=[sA], writes=[sB], inc=cv)
        p.emit("vector", lambda v: v.tensor_scalar(sB.ap, sB.ap, 1.0 / 120.0,
                                                   -1.0 / 6.0, ALU.mult, ALU.add),
               reads=[sB], writes=[sB], inc=cv)
        p.emit("vector", lambda v: v.tensor_mul(sB.ap, sB.ap, sA.ap),
               reads=[sA, sB], writes=[sB], inc=cv)
        p.emit("vector", lambda v: v.tensor_mul(sB.ap, sB.ap, sA.ap),
               reads=[sA, sB], writes=[sB], inc=cv)
        p.emit("vector", lambda v: v.tensor_mul(sB.ap, sB.ap, sA.ap),
               reads=[sA, sB], writes=[sB], inc=cv)
        p.emit("vector", lambda v: v.tensor_add(sB.ap, sB.ap, sA.ap),
               reads=[sA, sB], writes=[sB], inc=cv)
        dstws = ws_scr.rearrange("(c p) j -> p c j", p=128)
        p.emit("sync", lambda s: s.dma_start(out=dstws, in_=vB), reads=[sB], inc=d_w)

        wm1 = Buf(big1[:, 0:6 * D], "wm1")
        wm2 = Buf(big1[:, 6 * D:12 * D], "wm2")
        mcol_ps = Buf(ps_all[:, 0:48], "mcol_ps")
        mcols_b = Buf(mcols_f[:], "mcols")
        p.emit("vector", lambda v: v.memset(mcols_f[:], 0.0),
               writes=[mcols_b], inc=cv)
        for kc in range(DC):
            wmb = (wm1, wm2)[kc % 2]
            src = wmod[kc * 128:(kc + 1) * 128, :]
            p.emit("gpsimd", lambda g, wmb=wmb, src=src:
                   g.dma_start(out=wmb.ap, in_=src), writes=[wmb], inc=d_g)
            for j in range(48):
                grp, dcj = j // DC, j % DC
                last = (j == 47)
                p.emit("tensor", lambda t, wmb=wmb, kc=kc, j=j, grp=grp, dcj=dcj:
                       t.matmul(mcol_ps.ap[:, j:j + 1],
                                wmb.ap[:, grp * D + dcj * 128:grp * D + (dcj + 1) * 128],
                                siob[:, kc:kc + 1],
                                start=True, stop=True),
                       reads=[wmb, siob_b], writes=[mcol_ps] if last else (),
                       war=() if j else [mcol_ps],
                       inc=ct if last else None)
            p.emit("vector", lambda v: v.tensor_add(mcols_f[:], mcols_f[:],
                                                    mcol_ps.ap),
                   reads=[mcol_ps], writes=[mcols_b], inc=cv)

        mc_b = Buf(mc_adj[:], "mc_adj")
        p.emit("vector", lambda v: v.tensor_scalar_add(
            mc_adj[:, 0:16:2], mcols_f[:, 8:16], 1.0),
            reads=[mcols_b], writes=[mc_b], inc=cv)
        p.emit("vector", lambda v: v.tensor_copy(mc_adj[:, 1:16:2], mcols_f[:, 0:8]),
               reads=[mcols_b], writes=[mc_b], inc=cv)
        p.emit("vector", lambda v: v.tensor_scalar_add(
            mc_adj[:, 16:32:2], mcols_f[:, 32:40], 1.0),
            reads=[mcols_b], writes=[mc_b], inc=cv)
        p.emit("vector", lambda v: v.tensor_copy(mc_adj[:, 17:32:2], mcols_f[:, 24:32]),
               reads=[mcols_b], writes=[mc_b], inc=cv)

        rows1_b = Buf(rows1[:], "rows1")
        rows2_b = Buf(rows2[:], "rows2")
        for ln_i, (rt, rb_) in enumerate(((rows1, rows1_b), (rows2, rows2_b))):
            for dc in range(DC):
                for r in range(2):
                    col = ln_i * 16 + 2 * dc + r
                    p.emit("sync", lambda s, rt=rt, r=r, dc=dc, col=col: s.dma_start(
                        out=rt[r:r + 1, dc * 128:(dc + 1) * 128],
                        in_=mc_adj[:, col:col + 1],
                        allow_slow_non_contiguous=True),
                        reads=[mc_b], writes=[rb_], inc=d_s)

        g_b = Buf(gcols[:], "gcols")
        for gi, base in enumerate((16, 40)):
            dst = gcols[:, gi * DC:(gi + 1) * DC]
            tmp = gcols[:, (2 + gi) * DC:(3 + gi) * DC]
            src = mcols_f[:, base:base + DC]
            p.emit("vector", lambda v, tmp=tmp, src=src:
                   v.tensor_scalar(tmp, src, 0.0, 2.0, ALU.is_ge, ALU.mult),
                   reads=[mcols_b], writes=[g_b], inc=cv)
            p.emit("vector", lambda v, tmp=tmp:
                   v.tensor_scalar_add(tmp, tmp, -1.0),
                   reads=[g_b], writes=[g_b], inc=cv)
            p.emit("scalar", lambda s, dst=dst, src=src:
                   s.activation(dst, src, AF.Abs),
                   reads=[mcols_b], writes=[g_b], inc=cs)
            p.emit("vector", lambda v, dst=dst:
                   v.tensor_scalar_max(dst, dst, 0.25),
                   reads=[g_b], writes=[g_b], inc=cv)
            p.emit("vector", lambda v, dst=dst, tmp=tmp:
                   v.tensor_mul(dst, dst, tmp),
                   reads=[g_b], writes=[g_b], inc=cv)

        ones_b = Buf(ones1[:], "ones1")
        p.emit("vector", lambda v: v.memset(ones1[:], 1.0), writes=[ones_b], inc=cv)
        onesr_b = Buf(ones_r[:], "ones_r")
        p.emit("vector", lambda v: v.memset(ones_r[:], 1.0), writes=[onesr_b], inc=cv)
        aba_b = Buf(ab_a[:], "ab_a")
        abbo_b = Buf(ab_bo[:], "ab_bo")
        p.emit("vector", lambda v: v.memset(ab_bo[0:2, :], 1.0),
               writes=[abbo_b], inc=cv)

        p.barrier()
        wx_b = Buf(big1[:, 0:DC * 1056], "wx")
        wxv = big1[:, 0:DC * 1056].rearrange("p (c j) -> p c j", c=DC)
        p.emit("gpsimd", lambda g: g.dma_start(
            out=wxv, in_=Wxp.rearrange("(c p) j -> p c j", p=128)),
            writes=[wx_b], inc=d_g)

        WX_END = DC * 1056
        xs_p = [Buf(big2[:, 0:8 * TB], "x_bf0"),
                Buf(big1[:, WX_END:WX_END + 8 * TB], "x_bf1")]
        h1_p = [Buf(big2[:, 8 * TB:16 * TB], "h1_0"),
                Buf(big1[:, WX_END + 8 * TB:WX_END + 16 * TB], "h1_1")]
        dls = Buf(big2[:, 16 * TB:24 * TB], "dl")
        uss = Buf(big2[:, 24 * TB:32 * TB], "u")
        xsq = Buf(big4[:, 0:8 * TB], "xsq")
        esb = Buf(big4[:, 8 * TB:9 * TB], "esb")
        r_ps0 = Buf(ps_all[:, 0:TB], "ps0")
        r_ps1 = Buf(ps_all[:, TB:2 * TB], "ps1")
        r_ps2 = Buf(ps_all[:, 2 * TB:3 * TB], "ps2")
        r_ps3 = Buf(ps_all[:, 3 * TB:4 * TB], "ps3")
        rowf_b = Buf(rowf[:], "rowf")
        bc_b = Buf(bc_sb[:], "bc")

        def ln_stats_and_coeffs(src_b):
            p.emit("vector", lambda v: v.tensor_mul(xsq.ap, src_b.ap, src_b.ap),
                   reads=[src_b], writes=[xsq], inc=cv)
            for dc in range(DC):
                for hh in range(2):
                    last = (dc == DC - 1 and hh == 1)
                    p.emit("tensor", lambda t, dc=dc, hh=hh: t.matmul(
                        r_ps0.ap[0:1, hh * 512:(hh + 1) * 512], ones1[:, 0:1],
                        src_b.ap[:, dc * TB + hh * 512:dc * TB + (hh + 1) * 512],
                        start=(dc == 0), stop=(dc == DC - 1)),
                        reads=[src_b, ones_b], writes=[r_ps0] if last else (),
                        war=() if (dc or hh) else [r_ps0],
                        inc=ct if last else None)
            for dc in range(DC):
                for hh in range(2):
                    last = (dc == DC - 1 and hh == 1)
                    p.emit("tensor", lambda t, dc=dc, hh=hh: t.matmul(
                        r_ps1.ap[0:1, hh * 512:(hh + 1) * 512], ones1[:, 0:1],
                        xsq.ap[:, dc * TB + hh * 512:dc * TB + (hh + 1) * 512],
                        start=(dc == 0), stop=(dc == DC - 1)),
                        reads=[xsq, ones_b], writes=[r_ps1] if last else (),
                        war=() if (dc or hh) else [r_ps1],
                        inc=ct if last else None)
            p.emit("vector", lambda v: v.tensor_scalar_mul(rowf[0:1, 0:TB],
                                                           r_ps0.ap[0:1, :], 1.0 / D),
                   reads=[r_ps0], writes=[rowf_b], inc=cv)
            p.emit("scalar", lambda s: s.activation(rowf[0:1, TB:2 * TB], rowf[0:1, 0:TB],
                                                    AF.Square),
                   reads=[rowf_b], writes=[rowf_b], inc=cs)
            p.emit("vector", lambda v: v.tensor_scalar(rowf[0:1, 2 * TB:3 * TB],
                                                       r_ps1.ap[0:1, :], 1.0 / D,
                                                       EPS, ALU.mult, ALU.add),
                   reads=[r_ps1], writes=[rowf_b], inc=cv)
            p.emit("vector", lambda v: v.tensor_sub(rowf[0:1, 2 * TB:3 * TB], rowf[0:1, 2 * TB:3 * TB],
                                                    rowf[0:1, TB:2 * TB]),
                   reads=[rowf_b], writes=[rowf_b], inc=cv)
            p.emit("scalar", lambda s: s.activation(rowf[0:1, 2 * TB:3 * TB], rowf[0:1, 2 * TB:3 * TB],
                                                    AF.Ln),
                   reads=[rowf_b], writes=[rowf_b], inc=cs)
            p.emit("vector", lambda v: v.tensor_scalar_mul(rowf[0:1, 2 * TB:3 * TB],
                                                           rowf[0:1, 2 * TB:3 * TB], -0.5),
                   reads=[rowf_b], writes=[rowf_b], inc=cv)
            p.emit("scalar", lambda s: s.activation(rowf[0:1, 2 * TB:3 * TB], rowf[0:1, 2 * TB:3 * TB],
                                                    AF.Exp),
                   reads=[rowf_b], writes=[rowf_b], inc=cs)
            p.emit("vector", lambda v: v.tensor_copy(ab_a[0:1, :], rowf[0:1, 2 * TB:3 * TB]),
                   reads=[rowf_b], writes=[aba_b], inc=cv)
            p.emit("vector", lambda v: v.scalar_tensor_tensor(
                ab_bo[0:1, :], rowf[0:1, 0:TB], -1.0, rowf[0:1, 2 * TB:3 * TB],
                ALU.mult, ALU.mult),
                reads=[rowf_b], writes=[abbo_b], inc=cv)

        def ln_apply(dst_b, src_b, rows_t, rows_tb):
            for dc in range(DC):
                sl128 = slice(dc * 128, (dc + 1) * 128)
                for hh in range(2):
                    h5 = slice(hh * 512, (hh + 1) * 512)
                    p.emit("tensor", lambda t, sl128=sl128, rows_t=rows_t, h5=h5:
                           t.matmul(r_ps0.ap[:, h5], rows_t[0:1, sl128],
                                    ab_a[0:1, h5], start=True, stop=True),
                           reads=[rows_tb, aba_b], writes=[r_ps0], inc=ct)
                    p.emit("tensor", lambda t, sl128=sl128, rows_t=rows_t, h5=h5:
                           t.matmul(r_ps1.ap[:, h5], rows_t[0:2, sl128],
                                    ab_bo[0:2, h5], start=True, stop=True),
                           reads=[rows_tb, abbo_b], writes=[r_ps1], inc=ct)
                sl = slice(dc * TB, (dc + 1) * TB)
                p.emit("vector", lambda v, sl=sl: v.tensor_mul(
                    dst_b.ap[:, sl], src_b.ap[:, sl], r_ps0.ap[:, :]),
                    reads=[src_b, r_ps0], writes=[dst_b], inc=cv)
                p.emit("vector", lambda v, sl=sl: v.tensor_add(
                    dst_b.ap[:, sl], dst_b.ap[:, sl], r_ps1.ap[:, :]),
                    reads=[r_ps1], writes=[dst_b], inc=cv)

        for tb in range(NTB):
            t0, t1 = tb * TB, (tb + 1) * TB
            xs, h1s = xs_p[tb % 2], h1_p[tb % 2]
            p.emit("gpsimd", lambda g, xs=xs, t0=t0, t1=t1: g.dma_start(
                out=xs.ap.rearrange("p (c j) -> p c j", c=DC),
                in_=xT.rearrange("(c p) t -> p c t", p=128)[:, :, t0:t1]),
                writes=[xs], inc=d_g)
            ln_stats_and_coeffs(xs)
            ln_apply(h1s, xs, rows1, rows1_b)
            for mc in range(9):
                w = 128 if mc < 8 else 2 * N_ST
                pj = (r_ps2, r_ps3)[mc % 2]
                for kc in range(DC):
                    for hh in range(2):
                        last = (kc == DC - 1 and hh == 1)
                        p.emit("tensor",
                               lambda t, mc=mc, kc=kc, w=w, pj=pj, hh=hh, h1s=h1s:
                               t.matmul(
                            pj.ap[0:w, hh * 512:(hh + 1) * 512],
                            big1[:, kc * 1056 + mc * 128:kc * 1056 + mc * 128 + w],
                            h1s.ap[:, kc * TB + hh * 512:kc * TB + (hh + 1) * 512],
                            start=(kc == 0), stop=(kc == DC - 1)),
                            reads=[wx_b, h1s], writes=[pj] if last else (),
                            war=() if (kc or hh) else [pj],
                            inc=ct if last else None)
                if mc < 8:
                    sl = slice(mc * TB, (mc + 1) * TB)
                    p.emit("scalar", lambda s, pj=pj, mc=mc: s.activation(
                        esb.ap, pj.ap, AF.Exp, bias=cons[:, 8 + mc:9 + mc]),
                        reads=[pj, bxpc_b], writes=[esb], inc=cs)
                    p.emit("scalar", lambda s, sl=sl: s.activation(
                        dls.ap[:, sl], esb.ap, AF.Ln, bias=1.0),
                        reads=[esb], writes=[dls], inc=cs)
                    p.emit("vector", lambda v, sl=sl, h1s=h1s: v.tensor_mul(
                        uss.ap[:, sl], dls.ap[:, sl], h1s.ap[:, sl]),
                        reads=[dls, h1s], writes=[uss], inc=cv)
                else:
                    p.emit("vector", lambda v, pj=pj, t0=t0, t1=t1:
                           v.tensor_single_scalar(
                        bc_sb[:, t0:t1], pj.ap[0:2 * N_ST, :],
                        cons[0:2 * N_ST, 24:25], ALU.add),
                        reads=[pj, bbc_b], writes=[bc_b], inc=cv)
            for (buf, scr, off) in ((h1s, h1_scr, 0), (dls, du_scr, 0),
                                    (uss, du_scr, L)):
                p.emit("sync", lambda s, buf=buf, scr=scr, t0=t0, t1=t1, off=off:
                       s.dma_start(
                    out=scr[:, :, off + t0:off + t1].transpose([1, 0, 2]),
                    in_=buf.ap.rearrange("p (c j) -> p c j", c=DC)),
                    reads=[buf], inc=d_w)

        p.barrier()
        y_dc = [Buf(big1[:, dc * L:(dc + 1) * L], f"y{dc}") for dc in range(DC)]
        brep = [Buf(big2[:, i * L:(i + 1) * L], f"brep{i}") for i in range(2)]
        crep = [Buf(big2[:, (2 + i) * L:(3 + i) * L], f"crep{i}") for i in range(2)]
        dustr = [Buf(big2[:, (4 + 2 * i) * L:(6 + 2 * i) * L], f"dustr{i}")
                 for i in range(2)]
        zg_t = Buf(wor[:, 0:2048], "zg")
        abar_t = Buf(big4[:, 0:L], "abar")
        abar = [abar_t, abar_t]
        bx_t = Buf(big4[:, L:2 * L], "bx")
        h_t = Buf(big4[:, 2 * L:3 * L], "h")
        brep_ps = Buf(ps_all[:, :], "brep_ps")

        for dc in range(DC):
            p.emit("vector", lambda v, dc=dc: v.memset(y_dc[dc].ap, 0.0),
                   writes=[y_dc[dc]], inc=cv)

        bcstg_b = [Buf(bcstg[0:1, 0:L], "bcstg"), Buf(bcstg[0:1, 0:L], "bcstg")]
        bcstg_b[1] = bcstg_b[0]
        for n in range(N_ST):
            bb, cb_ = brep[n % 2], crep[n % 2]
            for si, (rb, row) in enumerate(((bb, n), (cb_, N_ST + n))):
                stg = bcstg_b[si]
                p.emit("sync", lambda s, stg=stg, row=row: s.dma_start(
                    out=stg.ap, in_=bc_sb[row:row + 1, :]),
                    reads=[bc_b], writes=[stg], inc=d_s)
                for q in range(8):
                    last = (q == 7)
                    p.emit("tensor", lambda t, stg=stg, q=q: t.matmul(
                        brep_ps.ap[:, q * 512:(q + 1) * 512],
                        ones_r[0:1, :],
                        stg.ap[:, q * 512:(q + 1) * 512],
                        start=True, stop=True),
                        reads=[stg, onesr_b], writes=[brep_ps] if last else (),
                        war=() if q else [brep_ps],
                        inc=ct if last else None)
                p.emit("vector", lambda v, rb=rb: v.tensor_copy(rb.ap, brep_ps.ap),
                       reads=[brep_ps], writes=[rb], inc=cv)
            for dc in range(DC):
                du = dustr[dc % 2]
                p.emit("sync", lambda s, du=du, dc=dc: s.dma_start(
                    out=du.ap, in_=du_scr[dc]), writes=[du], inc=d_s)
                ab = abar[dc % 2]
                p.emit("scalar", lambda s, ab=ab, du=du, dc=dc, n=n: s.activation(
                    ab.ap, du.ap[:, 0:L], AF.Exp,
                    scale=a_sc[:, dc * N_ST + n:dc * N_ST + n + 1]),
                    reads=[du, asc_b], writes=[ab], inc=cs)
                p.emit("vector", lambda v, du=du, bb=bb: v.tensor_mul(
                    bx_t.ap, du.ap[:, L:2 * L], bb.ap),
                    reads=[du, bb], writes=[bx_t], inc=cv)
                p.emit("vector", lambda v, ab=ab: v.tensor_tensor_scan(
                    h_t.ap, ab.ap, bx_t.ap, 0.0, ALU.mult, ALU.add),
                    reads=[ab, bx_t], writes=[h_t], inc=cv)
                if (n * DC + dc) % 4 == 3:
                    for hf in range(2):
                        hs = slice(hf * 2048, (hf + 1) * 2048)
                        p.emit("gpsimd", lambda g, cb_=cb_, hs=hs: g.tensor_mul(
                            zg_t.ap, h_t.ap[:, hs], cb_.ap[:, hs]),
                            reads=[h_t, cb_], writes=[zg_t], inc=cg)
                        p.emit("gpsimd", lambda g, dc=dc, hs=hs: g.tensor_add(
                            y_dc[dc].ap[:, hs], y_dc[dc].ap[:, hs], zg_t.ap),
                            reads=[zg_t], writes=[y_dc[dc]], inc=cg)
                else:
                    p.emit("vector", lambda v, cb_=cb_: v.tensor_mul(
                        bx_t.ap, h_t.ap, cb_.ap),
                        reads=[h_t, cb_], writes=[bx_t], inc=cv)
                    p.emit("vector", lambda v, dc=dc: v.tensor_add(
                        y_dc[dc].ap, y_dc[dc].ap, bx_t.ap),
                        reads=[bx_t], writes=[y_dc[dc]], inc=cv)

        for dc in range(DC):
            du = dustr[dc % 2]
            hv = Buf(du.ap[:, 0:L], f"h1v{dc}")
            hv.last_write = du.last_write
            hv.readers = list(du.readers)
            p.emit("sync", lambda s, du=du, dc=dc: s.dma_start(
                out=du.ap[:, 0:L], in_=h1_scr[dc]),
                reads=[], writes=[du], inc=d_s)
            p.emit("vector", lambda v, du=du, dc=dc: v.scalar_tensor_tensor(
                y_dc[dc].ap, du.ap[:, 0:L], cons[:, dc:dc + 1], y_dc[dc].ap,
                ALU.mult, ALU.add),
                reads=[du, dp_b], writes=[y_dc[dc]], inc=cv)

        p.barrier()
        r1s = Buf(big2[:, 0:8 * TB], "r1")
        h2s = Buf(big2[:, 8 * TB:16 * TB], "h2")
        xb2 = Buf(big2[:, 16 * TB:24 * TB], "xb2")
        xsq3 = Buf(big2[:, 24 * TB:32 * TB], "xsq3")
        b4f = big4[:].bitcast(F32)
        sq1 = Buf(b4f[:, 0:TB], "sq1")
        sq2 = Buf(b4f[:, TB:2 * TB], "sq2")
        omg = [Buf(b4f[:, (2 + i) * TB:(3 + i) * TB], f"omg{i}") for i in range(2)]
        worb = [Buf(wor[:, i * DC * 128:(i + 1) * DC * 128], f"wor{i}") for i in range(2)]
        wcrb = [Buf(wcr[:, i * DC * 128:(i + 1) * DC * 128], f"wcr{i}") for i in range(2)]
        wsrb = [Buf(wsr[:, i * DC * 128:(i + 1) * DC * 128], f"wsr{i}") for i in range(2)]

        def load_wring(scr, rb, mc):
            src = scr.rearrange("(c p) j -> p c j", p=128)[:, :, mc * 128:(mc + 1) * 128]
            p.emit("sync", lambda s: s.dma_start(
                out=rb.ap.rearrange("p (c j) -> p c j", c=DC), in_=src),
                writes=[rb], inc=d_s)

        for tb in range(NTB):
            t0, t1 = tb * TB, (tb + 1) * TB
            p.emit("gpsimd", lambda g, t0=t0, t1=t1: g.dma_start(
                out=xb2.ap.rearrange("p (c j) -> p c j", c=DC),
                in_=xT.rearrange("(c p) t -> p c t", p=128)[:, :, t0:t1]),
                writes=[xb2], inc=d_g)
            for mc in range(DC):
                rb = worb[mc % 2]
                load_wring(wo_scr, rb, mc)
                pj = (r_ps0, r_ps1)[mc % 2]
                for kc in range(DC):
                    for hh in range(2):
                        last = (kc == DC - 1 and hh == 1)
                        p.emit("tensor",
                               lambda t, rb=rb, kc=kc, pj=pj, t0=t0, hh=hh:
                               t.matmul(pj.ap[:, hh * 512:(hh + 1) * 512],
                                        rb.ap[:, kc * 128:(kc + 1) * 128],
                                        big1[:, kc * L + t0 + hh * 512:
                                             kc * L + t0 + (hh + 1) * 512],
                                        start=(kc == 0), stop=(kc == DC - 1)),
                               reads=[rb, y_dc[kc]], writes=[pj] if last else (),
                               war=() if (kc or hh) else [pj],
                               inc=ct if last else None)
                sl = slice(mc * TB, (mc + 1) * TB)
                p.emit("vector", lambda v, pj=pj, mc=mc, sl=sl: v.tensor_scalar(
                    h2s.ap[:, sl], pj.ap, cons[:, 16 + mc:17 + mc],
                    gcols[:, mc:mc + 1], ALU.add, ALU.mult),
                    reads=[pj, boutc_b, g_b], writes=[h2s], inc=cv)
                p.emit("vector", lambda v, sl=sl: v.tensor_add(
                    r1s.ap[:, sl], h2s.ap[:, sl], xb2.ap[:, sl]),
                    reads=[h2s, xb2], writes=[r1s], inc=cv)
            p.emit("vector", lambda v: v.tensor_mul(xsq3.ap, r1s.ap, r1s.ap),
                   reads=[r1s], writes=[xsq3], inc=cv)
            for dc in range(DC):
                for hh in range(2):
                    last = (dc == DC - 1 and hh == 1)
                    p.emit("tensor", lambda t, dc=dc, hh=hh: t.matmul(
                        r_ps2.ap[0:1, hh * 512:(hh + 1) * 512], ones1[:, 0:1],
                        r1s.ap[:, dc * TB + hh * 512:dc * TB + (hh + 1) * 512],
                        start=(dc == 0), stop=(dc == DC - 1)),
                        reads=[r1s, ones_b], writes=[r_ps2] if last else (),
                        war=() if (dc or hh) else [r_ps2],
                        inc=ct if last else None)
            for dc in range(DC):
                for hh in range(2):
                    last = (dc == DC - 1 and hh == 1)
                    p.emit("tensor", lambda t, dc=dc, hh=hh: t.matmul(
                        r_ps3.ap[0:1, hh * 512:(hh + 1) * 512], ones1[:, 0:1],
                        xsq3.ap[:, dc * TB + hh * 512:dc * TB + (hh + 1) * 512],
                        start=(dc == 0), stop=(dc == DC - 1)),
                        reads=[xsq3, ones_b], writes=[r_ps3] if last else (),
                        war=() if (dc or hh) else [r_ps3],
                        inc=ct if last else None)
            p.emit("vector", lambda v: v.tensor_scalar_mul(rowf[0:1, 0:TB],
                                                           r_ps2.ap[0:1, :], 1.0 / D),
                   reads=[r_ps2], writes=[rowf_b], inc=cv)
            p.emit("scalar", lambda s: s.activation(rowf[0:1, TB:2 * TB], rowf[0:1, 0:TB],
                                                    AF.Square),
                   reads=[rowf_b], writes=[rowf_b], inc=cs)
            p.emit("vector", lambda v: v.tensor_scalar(rowf[0:1, 2 * TB:3 * TB],
                                                       r_ps3.ap[0:1, :], 1.0 / D,
                                                       EPS, ALU.mult, ALU.add),
                   reads=[r_ps3], writes=[rowf_b], inc=cv)
            p.emit("vector", lambda v: v.tensor_sub(rowf[0:1, 2 * TB:3 * TB], rowf[0:1, 2 * TB:3 * TB],
                                                    rowf[0:1, TB:2 * TB]),
                   reads=[rowf_b], writes=[rowf_b], inc=cv)
            p.emit("scalar", lambda s: s.activation(rowf[0:1, 2 * TB:3 * TB], rowf[0:1, 2 * TB:3 * TB],
                                                    AF.Sqrt),
                   reads=[rowf_b], writes=[rowf_b], inc=cs)
            p.emit("vector", lambda v: v.reciprocal(rowf[0:1, 2 * TB:3 * TB], rowf[0:1, 2 * TB:3 * TB]),
                   reads=[rowf_b], writes=[rowf_b], inc=cv)
            p.emit("vector", lambda v: v.tensor_copy(ab_a[0:1, :], rowf[0:1, 2 * TB:3 * TB]),
                   reads=[rowf_b], writes=[aba_b], inc=cv)
            p.emit("vector", lambda v: v.scalar_tensor_tensor(
                ab_bo[0:1, :], rowf[0:1, 0:TB], -1.0, rowf[0:1, 2 * TB:3 * TB],
                ALU.mult, ALU.mult),
                reads=[rowf_b], writes=[abbo_b], inc=cv)
            ln_apply(h2s, r1s, rows2, rows2_b)
            for mc in range(DC):
                rbc = wcrb[mc % 2]
                rbs = wsrb[mc % 2]
                load_wring(wc_scr, rbc, mc)
                load_wring(ws_scr, rbs, mc)
                for kc in range(DC):
                    for hh in range(2):
                        last = (kc == DC - 1 and hh == 1)
                        p.emit("tensor", lambda t, rbc=rbc, kc=kc, hh=hh: t.matmul(
                            r_ps0.ap[:, hh * 512:(hh + 1) * 512],
                            rbc.ap[:, kc * 128:(kc + 1) * 128],
                            h2s.ap[:, kc * TB + hh * 512:kc * TB + (hh + 1) * 512],
                            start=(kc == 0), stop=(kc == DC - 1)),
                            reads=[rbc, h2s], writes=[r_ps0] if last else (),
                            war=() if (kc or hh) else [r_ps0],
                            inc=ct if last else None)
                for kc in range(DC):
                    for hh in range(2):
                        last = (kc == DC - 1 and hh == 1)
                        p.emit("tensor", lambda t, rbs=rbs, kc=kc, hh=hh: t.matmul(
                            r_ps1.ap[:, hh * 512:(hh + 1) * 512],
                            rbs.ap[:, kc * 128:(kc + 1) * 128],
                            h2s.ap[:, kc * TB + hh * 512:kc * TB + (hh + 1) * 512],
                            start=(kc == 0), stop=(kc == DC - 1)),
                            reads=[rbs, h2s], writes=[r_ps1] if last else (),
                            war=() if (kc or hh) else [r_ps1],
                            inc=ct if last else None)
                p.emit("scalar", lambda s: s.activation(sq1.ap, r_ps0.ap, AF.Square),
                       reads=[r_ps0], writes=[sq1], inc=cs)
                p.emit("scalar", lambda s: s.activation(sq2.ap, r_ps1.ap, AF.Square),
                       reads=[r_ps1], writes=[sq2], inc=cs)
                p.emit("vector", lambda v: v.tensor_add(sq1.ap, sq1.ap, sq2.ap),
                       reads=[sq1, sq2], writes=[sq1], inc=cv)
                p.emit("scalar", lambda s: s.activation(sq2.ap, sq1.ap, AF.Sqrt),
                       reads=[sq1], writes=[sq2], inc=cs)
                ob = omg[mc % 2]
                sl = slice(mc * TB, (mc + 1) * TB)
                p.emit("vector", lambda v, ob=ob, mc=mc, sl=sl:
                       v.scalar_tensor_tensor(ob.ap, sq2.ap,
                                              gcols[:, DC + mc:DC + mc + 1],
                                              r1s.ap[:, sl], ALU.mult, ALU.add),
                       reads=[sq2, g_b, r1s], writes=[ob], inc=cv)
                p.emit("sync", lambda s, ob=ob, mc=mc, t0=t0, t1=t1: s.dma_start(
                    out=outT[mc * 128:(mc + 1) * 128, t0:t1], in_=ob.ap),
                    reads=[ob], inc=d_o)

        p.emit("sync", None, extra_waits=[d_o])
        p.build(block)
    return nc


_NC_CACHE = {}


def kernel(L0_attn_x, L0_attn_A_log, L0_attn_Dp, L0_attn_W_xproj, L0_attn_b_xproj,
           L0_attn_W_out, L0_attn_b_out, L0_ffn_theta, L0_ffn_phi,
           L0_ffn_theta_noise, L0_ffn_phi_noise, L0_c, L0_wmod):
    B = L0_attn_x.shape[0]
    f32 = np.float32
    cc = np.ascontiguousarray

    A_log_r = cc(np.asarray(L0_attn_A_log, f32).reshape(DC, 128, N_ST)
                 .transpose(1, 0, 2).reshape(128, DC * N_ST))
    Dp_c = cc(np.asarray(L0_attn_Dp, f32).reshape(DC, 128).T)
    bxp_c = cc(np.asarray(L0_attn_b_xproj[:D], f32).reshape(DC, 128).T)
    bbc_c = cc(np.asarray(L0_attn_b_xproj[D:], f32).reshape(2 * N_ST, 1))
    bout_c = cc(np.asarray(L0_attn_b_out, f32).reshape(DC, 128).T)

    if "nc" not in _NC_CACHE:
        _NC_CACHE["nc"] = build_program()
    nc = _NC_CACHE["nc"]

    in_maps = []
    for b in range(B):
        in_maps.append(dict(
            xT=cc(np.asarray(L0_attn_x[b], f32).T),
            A_log_r=A_log_r,
            Dp_c=Dp_c,
            Wxp=cc(np.asarray(L0_attn_W_xproj, f32)),
            bxp_c=bxp_c,
            bbc_c=bbc_c,
            Wout=cc(np.asarray(L0_attn_W_out, f32)),
            bout_c=bout_c,
            theta=cc(np.asarray(L0_ffn_theta, f32)),
            phi=cc(np.asarray(L0_ffn_phi, f32)),
            theta_n=cc(np.asarray(L0_ffn_theta_noise, f32)),
            phi_n=cc(np.asarray(L0_ffn_phi_noise, f32)),
            c_col=cc(np.asarray(L0_c[b], f32).reshape(DC, 128).T),
            wmod=cc(np.asarray(L0_wmod, f32)),
        ))

    res = run_bass_kernel_spmd(nc, in_maps, core_ids=list(range(B)))
    out = np.empty((B, L, D), np.float32)
    for b in range(B):
        out[b] = res.results[b]["outT"].T
    return out


# version 08
# speedup vs baseline: 1.0968x; 1.0968x over previous
import numpy as np
import ml_dtypes
from contextlib import ExitStack

import concourse.bass as bass
import concourse.mybir as mybir
from concourse.bass_utils import run_bass_kernel_spmd

F32 = mybir.dt.float32
BF16 = mybir.dt.bfloat16
AF = mybir.ActivationFunctionType
ALU = mybir.AluOpType

ENGINES = ("sync", "scalar", "vector", "tensor", "gpsimd")

D = 1024
L = 4096
N_ST = 16
DC = 8
TB = 1024
NTB = L // TB
EPS = 1e-5



class Chan:

    def __init__(self, handle, dma=False):
        self.h = handle
        self.dma = dma
        self.count = 0
        self.amt = 16 if dma else 1


class Buf:

    def __init__(self, ap, name=""):
        self.ap = ap
        self.name = name
        self.last_write = None
        self.readers = []


class Prog:
    def __init__(self, nc):
        self.nc = nc
        self.steps = {e: [] for e in ENGINES}
        self.hwm = {e: {} for e in ENGINES}
        self.all_chans = []

    def chan(self, es, name, dma=False):
        h = es.enter_context(self.nc.semaphore(name))
        c = Chan(h, dma=dma)
        self.all_chans.append(c)
        return c

    def emit(self, engine, fn, reads=(), writes=(), extra_waits=(), inc=None,
             war=()):
        need = {}

        def add(cc):
            if cc is None:
                return
            ch, v = cc
            if v <= 0:
                return
            if need.get(ch, 0) < v:
                need[ch] = v

        for b in reads:
            add(b.last_write)
        for b in list(writes) + list(war):
            add(b.last_write)
            for r in b.readers:
                add(r)
        for w in extra_waits:
            add(w if isinstance(w, tuple) else (w, w.count))

        if inc is not None and inc.dma:
            add((inc, inc.count))
        hw = self.hwm[engine]
        waits = []
        for ch, v in need.items():
            if hw.get(ch, 0) < v:
                waits.append((ch, v))
                hw[ch] = v
        if inc is not None:
            inc.count += inc.amt
        self.steps[engine].append((fn, waits, inc))
        tag = (inc, inc.count) if inc is not None else None
        for b in writes:
            b.last_write = tag
            b.readers = []
        if tag is not None:
            for b in reads:
                b.readers.append(tag)

    def barrier(self):
        for e in ENGINES:
            for ch in self.all_chans:
                if ch.count > 0 and self.hwm[e].get(ch, 0) < ch.count:
                    self.steps[e].append((None, [(ch, ch.count)], None))
                    self.hwm[e][ch] = ch.count

    def build(self, block):
        def mk(engine):
            def run(eng):
                for fn, waits, inc in self.steps[engine]:
                    for ch, v in waits:
                        eng.wait_ge(ch.h, v)
                    if fn is not None:
                        ins = fn(eng)
                        if inc is not None:
                            assert ins is not None
                            ins.then_inc(inc.h, inc.amt)
                    elif inc is not None:
                        eng.sem_inc(inc.h, inc.amt)
            return run
        block.sync(mk("sync"))
        block.scalar(mk("scalar"))
        block.vector(mk("vector"))
        block.tensor(mk("tensor"))
        block.gpsimd(mk("gpsimd"))



def build_program():
    nc = bass.Bass("TRN2", target_bir_lowering=False)

    xT = nc.dram_tensor("xT", [D, L], F32, kind="ExternalInput").ap()
    A_log_r = nc.dram_tensor("A_log_r", [128, DC * N_ST], F32, kind="ExternalInput").ap()
    Dp_c = nc.dram_tensor("Dp_c", [128, DC], F32, kind="ExternalInput").ap()
    Wxp = nc.dram_tensor("Wxp", [D, D + 2 * N_ST], F32, kind="ExternalInput").ap()
    bxp_c = nc.dram_tensor("bxp_c", [128, DC], F32, kind="ExternalInput").ap()
    bbc_c = nc.dram_tensor("bbc_c", [2 * N_ST, 1], F32, kind="ExternalInput").ap()
    Wout = nc.dram_tensor("Wout", [D, D], F32, kind="ExternalInput").ap()
    bout_c = nc.dram_tensor("bout_c", [128, DC], F32, kind="ExternalInput").ap()
    theta = nc.dram_tensor("theta", [D, D], F32, kind="ExternalInput").ap()
    phi = nc.dram_tensor("phi", [D, D], F32, kind="ExternalInput").ap()
    theta_n = nc.dram_tensor("theta_n", [D, D], F32, kind="ExternalInput").ap()
    phi_n = nc.dram_tensor("phi_n", [D, D], F32, kind="ExternalInput").ap()
    c_col = nc.dram_tensor("c_col", [128, DC], F32, kind="ExternalInput").ap()
    wmod = nc.dram_tensor("wmod", [D, 6 * D], F32, kind="ExternalInput").ap()

    outT = nc.dram_tensor("outT", [D, L], F32, kind="ExternalOutput").ap()

    h1_scr = nc.dram_tensor("h1_scr", [DC, 128, L], BF16).ap()
    du_scr = nc.dram_tensor("du_scr", [DC, 128, 2 * L], BF16).ap()
    wo_scr = nc.dram_tensor("wo_scr", [D, D], BF16).ap()
    wc_scr = nc.dram_tensor("wc_scr", [D, D], BF16).ap()
    ws_scr = nc.dram_tensor("ws_scr", [D, D], BF16).ap()

    es = ExitStack()
    with es:
        p = Prog(nc)

        def sb(name, shape, dt):
            return es.enter_context(nc.sbuf_tensor(name, shape, dt))

        big1 = sb("big1", [128, 8 * L], BF16)
        big2 = sb("big2", [128, 8 * L], BF16)
        big4 = sb("big4", [128, 3 * L], BF16)
        wor = sb("wor", [128, 2 * DC * 128], BF16)
        wcr = sb("wcr", [128, 2 * DC * 128], BF16)
        wsr = sb("wsr", [128, 2 * DC * 128], BF16)
        bc_sb = sb("bc_sb", [2 * N_ST, L], BF16)
        bcstg = sb("bcstg", [1, L], BF16)
        cons = sb("cons", [128, 64], F32)
        a_sc = sb("a_sc", [128, 128], F32)
        mc_adj = sb("mc_adj", [128, 32], BF16)
        rows1 = sb("rows1", [2, D], BF16)
        rows2 = sb("rows2", [2, D], BF16)
        ab_a = sb("ab_a", [1, TB], BF16)
        ab_bo = sb("ab_bo", [2, TB], BF16)
        rowf = sb("rowf", [1, 3 * TB], F32)
        ones1 = sb("ones1", [128, 1], BF16)
        ones_r = sb("ones_r", [1, 128], BF16)
        sio = sb("sio", [128, DC], F32)
        siob = sb("siob", [128, DC], BF16)
        gcols = sb("gcols", [128, 4 * DC], F32)
        mcols_f = sb("mcols_f", [128, 48], F32)

        ps_all = es.enter_context(nc.psum_tensor("ps_all", [128, 4 * TB], F32))

        d_g = p.chan(es, "d_g", dma=True)
        d_s = p.chan(es, "d_s", dma=True)
        d_w = p.chan(es, "d_w", dma=True)
        d_o = p.chan(es, "d_o", dma=True)
        cv = p.chan(es, "cv")
        cs = p.chan(es, "cs")
        ct = p.chan(es, "ct")
        cg = p.chan(es, "cg")

        block = es.enter_context(nc.Block())

        asc_b = Buf(a_sc[:], "a_sc")
        p.emit("sync", lambda s: s.dma_start(out=a_sc[:], in_=A_log_r),
               writes=[asc_b], inc=d_s)
        p.emit("scalar", lambda s: s.activation(a_sc[:], a_sc[:], AF.Exp),
               reads=[asc_b], writes=[asc_b], inc=cs)
        p.emit("vector", lambda v: v.tensor_scalar_mul(a_sc[:], a_sc[:], -1.0),
               reads=[asc_b], writes=[asc_b], inc=cv)

        dp_b = Buf(cons[:, 0:DC], "dp")
        bxpc_b = Buf(cons[:, 8:8 + DC], "bxp")
        boutc_b = Buf(cons[:, 16:16 + DC], "bout")
        bbc_b = Buf(cons[0:2 * N_ST, 24:25], "bbc")
        p.emit("sync", lambda s: s.dma_start(out=cons[:, 0:DC], in_=Dp_c),
               writes=[dp_b], inc=d_s)
        p.emit("sync", lambda s: s.dma_start(out=cons[:, 8:8 + DC], in_=bxp_c),
               writes=[bxpc_b], inc=d_s)
        p.emit("sync", lambda s: s.dma_start(out=cons[:, 16:16 + DC], in_=bout_c),
               writes=[boutc_b], inc=d_s)
        p.emit("sync", lambda s: s.dma_start(out=cons[0:2 * N_ST, 24:25], in_=bbc_c),
               writes=[bbc_b], inc=d_s)

        sio_b = Buf(sio[:], "sio")
        siob_b = Buf(siob[:], "siob")
        p.emit("sync", lambda s: s.dma_start(out=sio[:], in_=c_col),
               writes=[sio_b], inc=d_s)
        p.emit("scalar", lambda s: s.activation(sio[:], sio[:], AF.Silu),
               reads=[sio_b], writes=[sio_b], inc=cs)
        p.emit("vector", lambda v: v.tensor_copy(siob[:], sio[:]),
               reads=[sio_b], writes=[siob_b], inc=cv)

        stg_a = Buf(big2[:, 0:8 * D], "stg_a")
        stg_b = Buf(big2[:, 8 * D:16 * D], "stg_b")
        stg_c = Buf(big2[:, 16 * D:24 * D], "stg_c")
        stg_d = Buf(big2[:, 24 * D:32 * D], "stg_d")

        def cast_weight_to_scr(w_in, w_out, stg):
            src = w_in.rearrange("(c p) j -> p c j", p=128)
            dst = w_out.rearrange("(c p) j -> p c j", p=128)
            sview = stg.ap.rearrange("p (c j) -> p c j", c=DC)
            p.emit("gpsimd", lambda g: g.dma_start(out=sview, in_=src),
                   writes=[stg], inc=d_g)
            p.emit("sync", lambda s: s.dma_start(out=dst, in_=sview),
                   reads=[stg], inc=d_w)

        cast_weight_to_scr(Wout, wo_scr, stg_a)

        def build_trig(w1, w2, w_out, stg1, stg2, is_cos):
            src1 = w1.rearrange("(c p) j -> p c j", p=128)
            src2 = w2.rearrange("(c p) j -> p c j", p=128)
            s1 = stg1.ap
            s2 = stg2.ap
            v1 = s1.rearrange("p (c j) -> p c j", c=DC)
            v2 = s2.rearrange("p (c j) -> p c j", c=DC)
            p.emit("gpsimd", lambda g: g.dma_start(out=v1, in_=src1),
                   writes=[stg1], inc=d_g)
            p.emit("gpsimd", lambda g: g.dma_start(out=v2, in_=src2),
                   writes=[stg2], inc=d_g)
            p.emit("vector", lambda v: v.tensor_add(s1, s1, s2),
                   reads=[stg1, stg2], writes=[stg1], inc=cv)
            p.emit("vector", lambda v: v.tensor_mul(s2, s1, s1),
                   reads=[stg1], writes=[stg2], inc=cv)
            if is_cos:
                p.emit("vector", lambda v: v.tensor_scalar(s2, s2, 1.0 / 24.0, -0.5,
                                                           ALU.mult, ALU.add),
                       reads=[stg2], writes=[stg2], inc=cv)
                p.emit("vector", lambda v: v.tensor_mul(s2, s2, s1),
                       reads=[stg1, stg2], writes=[stg2], inc=cv)
                p.emit("vector", lambda v: v.tensor_mul(s2, s2, s1),
                       reads=[stg1, stg2], writes=[stg2], inc=cv)
                p.emit("vector", lambda v: v.tensor_scalar_add(s1, s2, 1.0),
                       reads=[stg2], writes=[stg1], inc=cv)
            dst = w_out.rearrange("(c p) j -> p c j", p=128)
            sview = s1.rearrange("p (c j) -> p c j", c=DC)
            p.emit("sync", lambda s: s.dma_start(out=dst, in_=sview),
                   reads=[stg1], inc=d_w)

        build_trig(theta, theta_n, wc_scr, stg_a, stg_b, True)

        srcp = phi.rearrange("(c p) j -> p c j", p=128)
        srcpn = phi_n.rearrange("(c p) j -> p c j", p=128)
        sA, sB = stg_c, stg_d
        vA = sA.ap.rearrange("p (c j) -> p c j", c=DC)
        vB = sB.ap.rearrange("p (c j) -> p c j", c=DC)
        p.emit("gpsimd", lambda g: g.dma_start(out=vA, in_=srcp), writes=[sA], inc=d_g)
        p.emit("gpsimd", lambda g: g.dma_start(out=vB, in_=srcpn), writes=[sB], inc=d_g)
        p.emit("vector", lambda v: v.tensor_add(sA.ap, sA.ap, sB.ap),
               reads=[sA, sB], writes=[sA], inc=cv)
        p.emit("vector", lambda v: v.tensor_mul(sB.ap, sA.ap, sA.ap),
               reads=[sA], writes=[sB], inc=cv)
        p.emit("vector", lambda v: v.tensor_scalar(sB.ap, sB.ap, 1.0 / 120.0,
                                                   -1.0 / 6.0, ALU.mult, ALU.add),
               reads=[sB], writes=[sB], inc=cv)
        p.emit("vector", lambda v: v.tensor_mul(sB.ap, sB.ap, sA.ap),
               reads=[sA, sB], writes=[sB], inc=cv)
        p.emit("vector", lambda v: v.tensor_mul(sB.ap, sB.ap, sA.ap),
               reads=[sA, sB], writes=[sB], inc=cv)
        p.emit("vector", lambda v: v.tensor_mul(sB.ap, sB.ap, sA.ap),
               reads=[sA, sB], writes=[sB], inc=cv)
        p.emit("vector", lambda v: v.tensor_add(sB.ap, sB.ap, sA.ap),
               reads=[sA, sB], writes=[sB], inc=cv)
        dstws = ws_scr.rearrange("(c p) j -> p c j", p=128)
        p.emit("sync", lambda s: s.dma_start(out=dstws, in_=vB), reads=[sB], inc=d_w)

        wm1 = Buf(big1[:, 0:6 * D], "wm1")
        wm2 = Buf(big1[:, 6 * D:12 * D], "wm2")
        mcol_ps = Buf(ps_all[:, 0:48], "mcol_ps")
        mcols_b = Buf(mcols_f[:], "mcols")
        p.emit("vector", lambda v: v.memset(mcols_f[:], 0.0),
               writes=[mcols_b], inc=cv)
        for kc in range(DC):
            wmb = (wm1, wm2)[kc % 2]
            src = wmod[kc * 128:(kc + 1) * 128, :]
            p.emit("gpsimd", lambda g, wmb=wmb, src=src:
                   g.dma_start(out=wmb.ap, in_=src), writes=[wmb], inc=d_g)
            for j in range(48):
                grp, dcj = j // DC, j % DC
                last = (j == 47)
                p.emit("tensor", lambda t, wmb=wmb, kc=kc, j=j, grp=grp, dcj=dcj:
                       t.matmul(mcol_ps.ap[:, j:j + 1],
                                wmb.ap[:, grp * D + dcj * 128:grp * D + (dcj + 1) * 128],
                                siob[:, kc:kc + 1],
                                start=True, stop=True),
                       reads=[wmb, siob_b], writes=[mcol_ps] if last else (),
                       war=() if j else [mcol_ps],
                       inc=ct if last else None)
            p.emit("vector", lambda v: v.tensor_add(mcols_f[:], mcols_f[:],
                                                    mcol_ps.ap),
                   reads=[mcol_ps], writes=[mcols_b], inc=cv)

        mc_b = Buf(mc_adj[:], "mc_adj")
        p.emit("vector", lambda v: v.tensor_scalar_add(
            mc_adj[:, 0:16:2], mcols_f[:, 8:16], 1.0),
            reads=[mcols_b], writes=[mc_b], inc=cv)
        p.emit("vector", lambda v: v.tensor_copy(mc_adj[:, 1:16:2], mcols_f[:, 0:8]),
               reads=[mcols_b], writes=[mc_b], inc=cv)
        p.emit("vector", lambda v: v.tensor_scalar_add(
            mc_adj[:, 16:32:2], mcols_f[:, 32:40], 1.0),
            reads=[mcols_b], writes=[mc_b], inc=cv)
        p.emit("vector", lambda v: v.tensor_copy(mc_adj[:, 17:32:2], mcols_f[:, 24:32]),
               reads=[mcols_b], writes=[mc_b], inc=cv)

        rows1_b = Buf(rows1[:], "rows1")
        rows2_b = Buf(rows2[:], "rows2")
        for ln_i, (rt, rb_) in enumerate(((rows1, rows1_b), (rows2, rows2_b))):
            for dc in range(DC):
                for r in range(2):
                    col = ln_i * 16 + 2 * dc + r
                    p.emit("sync", lambda s, rt=rt, r=r, dc=dc, col=col: s.dma_start(
                        out=rt[r:r + 1, dc * 128:(dc + 1) * 128],
                        in_=mc_adj[:, col:col + 1],
                        allow_slow_non_contiguous=True),
                        reads=[mc_b], writes=[rb_], inc=d_s)

        g_b = Buf(gcols[:], "gcols")
        for gi, base in enumerate((16, 40)):
            dst = gcols[:, gi * DC:(gi + 1) * DC]
            tmp = gcols[:, (2 + gi) * DC:(3 + gi) * DC]
            src = mcols_f[:, base:base + DC]
            p.emit("vector", lambda v, tmp=tmp, src=src:
                   v.tensor_scalar(tmp, src, 0.0, 2.0, ALU.is_ge, ALU.mult),
                   reads=[mcols_b], writes=[g_b], inc=cv)
            p.emit("vector", lambda v, tmp=tmp:
                   v.tensor_scalar_add(tmp, tmp, -1.0),
                   reads=[g_b], writes=[g_b], inc=cv)
            p.emit("scalar", lambda s, dst=dst, src=src:
                   s.activation(dst, src, AF.Abs),
                   reads=[mcols_b], writes=[g_b], inc=cs)
            p.emit("vector", lambda v, dst=dst:
                   v.tensor_scalar_max(dst, dst, 0.25),
                   reads=[g_b], writes=[g_b], inc=cv)
            p.emit("vector", lambda v, dst=dst, tmp=tmp:
                   v.tensor_mul(dst, dst, tmp),
                   reads=[g_b], writes=[g_b], inc=cv)

        ones_b = Buf(ones1[:], "ones1")
        p.emit("vector", lambda v: v.memset(ones1[:], 1.0), writes=[ones_b], inc=cv)
        onesr_b = Buf(ones_r[:], "ones_r")
        p.emit("vector", lambda v: v.memset(ones_r[:], 1.0), writes=[onesr_b], inc=cv)
        aba_b = Buf(ab_a[:], "ab_a")
        abbo_b = Buf(ab_bo[:], "ab_bo")
        p.emit("vector", lambda v: v.memset(ab_bo[0:2, :], 1.0),
               writes=[abbo_b], inc=cv)

        p.barrier()
        wx_b = Buf(big1[:, 0:DC * 1056], "wx")
        wxv = big1[:, 0:DC * 1056].rearrange("p (c j) -> p c j", c=DC)
        p.emit("gpsimd", lambda g: g.dma_start(
            out=wxv, in_=Wxp.rearrange("(c p) j -> p c j", p=128)),
            writes=[wx_b], inc=d_g)

        WX_END = DC * 1056
        xs_p = [Buf(big2[:, 0:8 * TB], "x_bf0"),
                Buf(big1[:, WX_END:WX_END + 8 * TB], "x_bf1")]
        h1_p = [Buf(big2[:, 8 * TB:16 * TB], "h1_0"),
                Buf(big1[:, WX_END + 8 * TB:WX_END + 16 * TB], "h1_1")]
        dls = Buf(big2[:, 16 * TB:24 * TB], "dl")
        uss = Buf(big2[:, 24 * TB:32 * TB], "u")
        xsq = Buf(big4[:, 0:8 * TB], "xsq")
        esb = Buf(big4[:, 8 * TB:9 * TB], "esb")
        r_ps0 = Buf(ps_all[:, 0:TB], "ps0")
        r_ps1 = Buf(ps_all[:, TB:2 * TB], "ps1")
        r_ps2 = Buf(ps_all[:, 2 * TB:3 * TB], "ps2")
        r_ps3 = Buf(ps_all[:, 3 * TB:4 * TB], "ps3")
        rowf_b = Buf(rowf[:], "rowf")
        bc_b = Buf(bc_sb[:], "bc")

        def ln_stats_and_coeffs(src_b):
            p.emit("vector", lambda v: v.tensor_mul(xsq.ap, src_b.ap, src_b.ap),
                   reads=[src_b], writes=[xsq], inc=cv)
            for dc in range(DC):
                for hh in range(2):
                    last = (dc == DC - 1 and hh == 1)
                    p.emit("tensor", lambda t, dc=dc, hh=hh: t.matmul(
                        r_ps0.ap[0:1, hh * 512:(hh + 1) * 512], ones1[:, 0:1],
                        src_b.ap[:, dc * TB + hh * 512:dc * TB + (hh + 1) * 512],
                        start=(dc == 0), stop=(dc == DC - 1)),
                        reads=[src_b, ones_b], writes=[r_ps0] if last else (),
                        war=() if (dc or hh) else [r_ps0],
                        inc=ct if last else None)
            for dc in range(DC):
                for hh in range(2):
                    last = (dc == DC - 1 and hh == 1)
                    p.emit("tensor", lambda t, dc=dc, hh=hh: t.matmul(
                        r_ps1.ap[0:1, hh * 512:(hh + 1) * 512], ones1[:, 0:1],
                        xsq.ap[:, dc * TB + hh * 512:dc * TB + (hh + 1) * 512],
                        start=(dc == 0), stop=(dc == DC - 1)),
                        reads=[xsq, ones_b], writes=[r_ps1] if last else (),
                        war=() if (dc or hh) else [r_ps1],
                        inc=ct if last else None)
            p.emit("vector", lambda v: v.tensor_scalar_mul(rowf[0:1, 0:TB],
                                                           r_ps0.ap[0:1, :], 1.0 / D),
                   reads=[r_ps0], writes=[rowf_b], inc=cv)
            p.emit("scalar", lambda s: s.activation(rowf[0:1, TB:2 * TB], rowf[0:1, 0:TB],
                                                    AF.Square),
                   reads=[rowf_b], writes=[rowf_b], inc=cs)
            p.emit("vector", lambda v: v.tensor_scalar(rowf[0:1, 2 * TB:3 * TB],
                                                       r_ps1.ap[0:1, :], 1.0 / D,
                                                       EPS, ALU.mult, ALU.add),
                   reads=[r_ps1], writes=[rowf_b], inc=cv)
            p.emit("vector", lambda v: v.tensor_sub(rowf[0:1, 2 * TB:3 * TB], rowf[0:1, 2 * TB:3 * TB],
                                                    rowf[0:1, TB:2 * TB]),
                   reads=[rowf_b], writes=[rowf_b], inc=cv)
            p.emit("scalar", lambda s: s.activation(rowf[0:1, 2 * TB:3 * TB], rowf[0:1, 2 * TB:3 * TB],
                                                    AF.Ln),
                   reads=[rowf_b], writes=[rowf_b], inc=cs)
            p.emit("vector", lambda v: v.tensor_scalar_mul(rowf[0:1, 2 * TB:3 * TB],
                                                           rowf[0:1, 2 * TB:3 * TB], -0.5),
                   reads=[rowf_b], writes=[rowf_b], inc=cv)
            p.emit("scalar", lambda s: s.activation(rowf[0:1, 2 * TB:3 * TB], rowf[0:1, 2 * TB:3 * TB],
                                                    AF.Exp),
                   reads=[rowf_b], writes=[rowf_b], inc=cs)
            p.emit("vector", lambda v: v.tensor_copy(ab_a[0:1, :], rowf[0:1, 2 * TB:3 * TB]),
                   reads=[rowf_b], writes=[aba_b], inc=cv)
            p.emit("vector", lambda v: v.scalar_tensor_tensor(
                ab_bo[0:1, :], rowf[0:1, 0:TB], -1.0, rowf[0:1, 2 * TB:3 * TB],
                ALU.mult, ALU.mult),
                reads=[rowf_b], writes=[abbo_b], inc=cv)

        def ln_apply(dst_b, src_b, rows_t, rows_tb):
            for dc in range(DC):
                sl128 = slice(dc * 128, (dc + 1) * 128)
                for hh in range(2):
                    h5 = slice(hh * 512, (hh + 1) * 512)
                    p.emit("tensor", lambda t, sl128=sl128, rows_t=rows_t, h5=h5:
                           t.matmul(r_ps0.ap[:, h5], rows_t[0:1, sl128],
                                    ab_a[0:1, h5], start=True, stop=True),
                           reads=[rows_tb, aba_b], writes=[r_ps0], inc=ct)
                    p.emit("tensor", lambda t, sl128=sl128, rows_t=rows_t, h5=h5:
                           t.matmul(r_ps1.ap[:, h5], rows_t[0:2, sl128],
                                    ab_bo[0:2, h5], start=True, stop=True),
                           reads=[rows_tb, abbo_b], writes=[r_ps1], inc=ct)
                sl = slice(dc * TB, (dc + 1) * TB)
                p.emit("vector", lambda v, sl=sl: v.tensor_mul(
                    dst_b.ap[:, sl], src_b.ap[:, sl], r_ps0.ap[:, :]),
                    reads=[src_b, r_ps0], writes=[dst_b], inc=cv)
                p.emit("vector", lambda v, sl=sl: v.tensor_add(
                    dst_b.ap[:, sl], dst_b.ap[:, sl], r_ps1.ap[:, :]),
                    reads=[r_ps1], writes=[dst_b], inc=cv)

        for tb in range(NTB):
            t0, t1 = tb * TB, (tb + 1) * TB
            xs, h1s = xs_p[tb % 2], h1_p[tb % 2]
            p.emit("gpsimd", lambda g, xs=xs, t0=t0, t1=t1: g.dma_start(
                out=xs.ap.rearrange("p (c j) -> p c j", c=DC),
                in_=xT.rearrange("(c p) t -> p c t", p=128)[:, :, t0:t1]),
                writes=[xs], inc=d_g)
            ln_stats_and_coeffs(xs)
            ln_apply(h1s, xs, rows1, rows1_b)
            for mc in range(9):
                w = 128 if mc < 8 else 2 * N_ST
                pj = (r_ps2, r_ps3)[mc % 2]
                for kc in range(DC):
                    for hh in range(2):
                        last = (kc == DC - 1 and hh == 1)
                        p.emit("tensor",
                               lambda t, mc=mc, kc=kc, w=w, pj=pj, hh=hh, h1s=h1s:
                               t.matmul(
                            pj.ap[0:w, hh * 512:(hh + 1) * 512],
                            big1[:, kc * 1056 + mc * 128:kc * 1056 + mc * 128 + w],
                            h1s.ap[:, kc * TB + hh * 512:kc * TB + (hh + 1) * 512],
                            start=(kc == 0), stop=(kc == DC - 1)),
                            reads=[wx_b, h1s], writes=[pj] if last else (),
                            war=() if (kc or hh) else [pj],
                            inc=ct if last else None)
                if mc < 8:
                    sl = slice(mc * TB, (mc + 1) * TB)
                    p.emit("scalar", lambda s, pj=pj, mc=mc: s.activation(
                        esb.ap, pj.ap, AF.Exp, bias=cons[:, 8 + mc:9 + mc]),
                        reads=[pj, bxpc_b], writes=[esb], inc=cs)
                    p.emit("scalar", lambda s, sl=sl: s.activation(
                        dls.ap[:, sl], esb.ap, AF.Ln, bias=1.0),
                        reads=[esb], writes=[dls], inc=cs)
                    p.emit("vector", lambda v, sl=sl, h1s=h1s: v.tensor_mul(
                        uss.ap[:, sl], dls.ap[:, sl], h1s.ap[:, sl]),
                        reads=[dls, h1s], writes=[uss], inc=cv)
                else:
                    p.emit("vector", lambda v, pj=pj, t0=t0, t1=t1:
                           v.tensor_single_scalar(
                        bc_sb[:, t0:t1], pj.ap[0:2 * N_ST, :],
                        cons[0:2 * N_ST, 24:25], ALU.add),
                        reads=[pj, bbc_b], writes=[bc_b], inc=cv)
            for (buf, scr, off) in ((h1s, h1_scr, 0), (dls, du_scr, 0),
                                    (uss, du_scr, L)):
                p.emit("sync", lambda s, buf=buf, scr=scr, t0=t0, t1=t1, off=off:
                       s.dma_start(
                    out=scr[:, :, off + t0:off + t1].transpose([1, 0, 2]),
                    in_=buf.ap.rearrange("p (c j) -> p c j", c=DC)),
                    reads=[buf], inc=d_w)

        p.barrier()
        y_dc = [Buf(big1[:, dc * L:(dc + 1) * L], f"y{dc}") for dc in range(DC)]
        brep = [Buf(big2[:, i * L:(i + 1) * L], f"brep{i}") for i in range(2)]
        crep = [Buf(big2[:, (2 + i) * L:(3 + i) * L], f"crep{i}") for i in range(2)]
        dustr = [Buf(big2[:, (4 + 2 * i) * L:(6 + 2 * i) * L], f"dustr{i}")
                 for i in range(2)]
        zg_t = Buf(wor[:, 0:2048], "zg")
        abar_t = Buf(big4[:, 0:L], "abar")
        abar = [abar_t, abar_t]
        bx_t = Buf(big4[:, L:2 * L], "bx")
        h_t = Buf(big4[:, 2 * L:3 * L], "h")
        brep_ps = Buf(ps_all[:, :], "brep_ps")

        for dc in range(DC):
            p.emit("vector", lambda v, dc=dc: v.memset(y_dc[dc].ap, 0.0),
                   writes=[y_dc[dc]], inc=cv)

        bcstg_b = [Buf(bcstg[0:1, 0:L], "bcstg"), Buf(bcstg[0:1, 0:L], "bcstg")]
        bcstg_b[1] = bcstg_b[0]
        for n in range(N_ST):
            bb, cb_ = brep[n % 2], crep[n % 2]
            for si, (rb, row) in enumerate(((bb, n), (cb_, N_ST + n))):
                stg = bcstg_b[si]
                p.emit("sync", lambda s, stg=stg, row=row: s.dma_start(
                    out=stg.ap, in_=bc_sb[row:row + 1, :]),
                    reads=[bc_b], writes=[stg], inc=d_s)
                for q in range(8):
                    last = (q == 7)
                    p.emit("tensor", lambda t, stg=stg, q=q: t.matmul(
                        brep_ps.ap[:, q * 512:(q + 1) * 512],
                        ones_r[0:1, :],
                        stg.ap[:, q * 512:(q + 1) * 512],
                        start=True, stop=True),
                        reads=[stg, onesr_b], writes=[brep_ps] if last else (),
                        war=() if q else [brep_ps],
                        inc=ct if last else None)
                p.emit("scalar", lambda s, rb=rb: s.activation(rb.ap, brep_ps.ap,
                                                               AF.Copy),
                       reads=[brep_ps], writes=[rb], inc=cs)
            for dc in range(DC):
                du = dustr[dc % 2]
                p.emit("sync", lambda s, du=du, dc=dc: s.dma_start(
                    out=du.ap, in_=du_scr[dc]), writes=[du], inc=d_s)
                ab = abar[dc % 2]
                p.emit("scalar", lambda s, ab=ab, du=du, dc=dc, n=n: s.activation(
                    ab.ap, du.ap[:, 0:L], AF.Exp,
                    scale=a_sc[:, dc * N_ST + n:dc * N_ST + n + 1]),
                    reads=[du, asc_b], writes=[ab], inc=cs)
                p.emit("vector", lambda v, du=du, bb=bb: v.tensor_mul(
                    bx_t.ap, du.ap[:, L:2 * L], bb.ap),
                    reads=[du, bb], writes=[bx_t], inc=cv)
                p.emit("vector", lambda v, ab=ab: v.tensor_tensor_scan(
                    h_t.ap, ab.ap, bx_t.ap, 0.0, ALU.mult, ALU.add),
                    reads=[ab, bx_t], writes=[h_t], inc=cv)
                p.emit("vector", lambda v, cb_=cb_: v.tensor_mul(
                    bx_t.ap, h_t.ap, cb_.ap),
                    reads=[h_t, cb_], writes=[bx_t], inc=cv)
                p.emit("vector", lambda v, dc=dc: v.tensor_add(
                    y_dc[dc].ap, y_dc[dc].ap, bx_t.ap),
                    reads=[bx_t], writes=[y_dc[dc]], inc=cv)

        for dc in range(DC):
            du = dustr[dc % 2]
            hv = Buf(du.ap[:, 0:L], f"h1v{dc}")
            hv.last_write = du.last_write
            hv.readers = list(du.readers)
            p.emit("sync", lambda s, du=du, dc=dc: s.dma_start(
                out=du.ap[:, 0:L], in_=h1_scr[dc]),
                reads=[], writes=[du], inc=d_s)
            p.emit("vector", lambda v, du=du, dc=dc: v.scalar_tensor_tensor(
                y_dc[dc].ap, du.ap[:, 0:L], cons[:, dc:dc + 1], y_dc[dc].ap,
                ALU.mult, ALU.add),
                reads=[du, dp_b], writes=[y_dc[dc]], inc=cv)

        p.barrier()
        r1s = Buf(big2[:, 0:8 * TB], "r1")
        h2s = Buf(big2[:, 8 * TB:16 * TB], "h2")
        xb2 = Buf(big2[:, 16 * TB:24 * TB], "xb2")
        xsq3 = Buf(big2[:, 24 * TB:32 * TB], "xsq3")
        b4f = big4[:].bitcast(F32)
        sq1 = Buf(b4f[:, 0:TB], "sq1")
        sq2 = Buf(b4f[:, TB:2 * TB], "sq2")
        omg = [Buf(b4f[:, (2 + i) * TB:(3 + i) * TB], f"omg{i}") for i in range(2)]
        worb = [Buf(wor[:, i * DC * 128:(i + 1) * DC * 128], f"wor{i}") for i in range(2)]
        wcrb = [Buf(wcr[:, i * DC * 128:(i + 1) * DC * 128], f"wcr{i}") for i in range(2)]
        wsrb = [Buf(wsr[:, i * DC * 128:(i + 1) * DC * 128], f"wsr{i}") for i in range(2)]

        def load_wring(scr, rb, mc):
            src = scr.rearrange("(c p) j -> p c j", p=128)[:, :, mc * 128:(mc + 1) * 128]
            p.emit("sync", lambda s: s.dma_start(
                out=rb.ap.rearrange("p (c j) -> p c j", c=DC), in_=src),
                writes=[rb], inc=d_s)

        for tb in range(NTB):
            t0, t1 = tb * TB, (tb + 1) * TB
            p.emit("gpsimd", lambda g, t0=t0, t1=t1: g.dma_start(
                out=xb2.ap.rearrange("p (c j) -> p c j", c=DC),
                in_=xT.rearrange("(c p) t -> p c t", p=128)[:, :, t0:t1]),
                writes=[xb2], inc=d_g)
            for mc in range(DC):
                rb = worb[mc % 2]
                load_wring(wo_scr, rb, mc)
                pj = (r_ps0, r_ps1)[mc % 2]
                for kc in range(DC):
                    for hh in range(2):
                        last = (kc == DC - 1 and hh == 1)
                        p.emit("tensor",
                               lambda t, rb=rb, kc=kc, pj=pj, t0=t0, hh=hh:
                               t.matmul(pj.ap[:, hh * 512:(hh + 1) * 512],
                                        rb.ap[:, kc * 128:(kc + 1) * 128],
                                        big1[:, kc * L + t0 + hh * 512:
                                             kc * L + t0 + (hh + 1) * 512],
                                        start=(kc == 0), stop=(kc == DC - 1)),
                               reads=[rb, y_dc[kc]], writes=[pj] if last else (),
                               war=() if (kc or hh) else [pj],
                               inc=ct if last else None)
                sl = slice(mc * TB, (mc + 1) * TB)
                p.emit("vector", lambda v, pj=pj, mc=mc, sl=sl: v.tensor_scalar(
                    h2s.ap[:, sl], pj.ap, cons[:, 16 + mc:17 + mc],
                    gcols[:, mc:mc + 1], ALU.add, ALU.mult),
                    reads=[pj, boutc_b, g_b], writes=[h2s], inc=cv)
                p.emit("vector", lambda v, sl=sl: v.tensor_add(
                    r1s.ap[:, sl], h2s.ap[:, sl], xb2.ap[:, sl]),
                    reads=[h2s, xb2], writes=[r1s], inc=cv)
            p.emit("vector", lambda v: v.tensor_mul(xsq3.ap, r1s.ap, r1s.ap),
                   reads=[r1s], writes=[xsq3], inc=cv)
            for dc in range(DC):
                for hh in range(2):
                    last = (dc == DC - 1 and hh == 1)
                    p.emit("tensor", lambda t, dc=dc, hh=hh: t.matmul(
                        r_ps2.ap[0:1, hh * 512:(hh + 1) * 512], ones1[:, 0:1],
                        r1s.ap[:, dc * TB + hh * 512:dc * TB + (hh + 1) * 512],
                        start=(dc == 0), stop=(dc == DC - 1)),
                        reads=[r1s, ones_b], writes=[r_ps2] if last else (),
                        war=() if (dc or hh) else [r_ps2],
                        inc=ct if last else None)
            for dc in range(DC):
                for hh in range(2):
                    last = (dc == DC - 1 and hh == 1)
                    p.emit("tensor", lambda t, dc=dc, hh=hh: t.matmul(
                        r_ps3.ap[0:1, hh * 512:(hh + 1) * 512], ones1[:, 0:1],
                        xsq3.ap[:, dc * TB + hh * 512:dc * TB + (hh + 1) * 512],
                        start=(dc == 0), stop=(dc == DC - 1)),
                        reads=[xsq3, ones_b], writes=[r_ps3] if last else (),
                        war=() if (dc or hh) else [r_ps3],
                        inc=ct if last else None)
            p.emit("vector", lambda v: v.tensor_scalar_mul(rowf[0:1, 0:TB],
                                                           r_ps2.ap[0:1, :], 1.0 / D),
                   reads=[r_ps2], writes=[rowf_b], inc=cv)
            p.emit("scalar", lambda s: s.activation(rowf[0:1, TB:2 * TB], rowf[0:1, 0:TB],
                                                    AF.Square),
                   reads=[rowf_b], writes=[rowf_b], inc=cs)
            p.emit("vector", lambda v: v.tensor_scalar(rowf[0:1, 2 * TB:3 * TB],
                                                       r_ps3.ap[0:1, :], 1.0 / D,
                                                       EPS, ALU.mult, ALU.add),
                   reads=[r_ps3], writes=[rowf_b], inc=cv)
            p.emit("vector", lambda v: v.tensor_sub(rowf[0:1, 2 * TB:3 * TB], rowf[0:1, 2 * TB:3 * TB],
                                                    rowf[0:1, TB:2 * TB]),
                   reads=[rowf_b], writes=[rowf_b], inc=cv)
            p.emit("scalar", lambda s: s.activation(rowf[0:1, 2 * TB:3 * TB], rowf[0:1, 2 * TB:3 * TB],
                                                    AF.Sqrt),
                   reads=[rowf_b], writes=[rowf_b], inc=cs)
            p.emit("vector", lambda v: v.reciprocal(rowf[0:1, 2 * TB:3 * TB], rowf[0:1, 2 * TB:3 * TB]),
                   reads=[rowf_b], writes=[rowf_b], inc=cv)
            p.emit("vector", lambda v: v.tensor_copy(ab_a[0:1, :], rowf[0:1, 2 * TB:3 * TB]),
                   reads=[rowf_b], writes=[aba_b], inc=cv)
            p.emit("vector", lambda v: v.scalar_tensor_tensor(
                ab_bo[0:1, :], rowf[0:1, 0:TB], -1.0, rowf[0:1, 2 * TB:3 * TB],
                ALU.mult, ALU.mult),
                reads=[rowf_b], writes=[abbo_b], inc=cv)
            ln_apply(h2s, r1s, rows2, rows2_b)
            for mc in range(DC):
                rbc = wcrb[mc % 2]
                rbs = wsrb[mc % 2]
                load_wring(wc_scr, rbc, mc)
                load_wring(ws_scr, rbs, mc)
                for kc in range(DC):
                    for hh in range(2):
                        last = (kc == DC - 1 and hh == 1)
                        p.emit("tensor", lambda t, rbc=rbc, kc=kc, hh=hh: t.matmul(
                            r_ps0.ap[:, hh * 512:(hh + 1) * 512],
                            rbc.ap[:, kc * 128:(kc + 1) * 128],
                            h2s.ap[:, kc * TB + hh * 512:kc * TB + (hh + 1) * 512],
                            start=(kc == 0), stop=(kc == DC - 1)),
                            reads=[rbc, h2s], writes=[r_ps0] if last else (),
                            war=() if (kc or hh) else [r_ps0],
                            inc=ct if last else None)
                for kc in range(DC):
                    for hh in range(2):
                        last = (kc == DC - 1 and hh == 1)
                        p.emit("tensor", lambda t, rbs=rbs, kc=kc, hh=hh: t.matmul(
                            r_ps1.ap[:, hh * 512:(hh + 1) * 512],
                            rbs.ap[:, kc * 128:(kc + 1) * 128],
                            h2s.ap[:, kc * TB + hh * 512:kc * TB + (hh + 1) * 512],
                            start=(kc == 0), stop=(kc == DC - 1)),
                            reads=[rbs, h2s], writes=[r_ps1] if last else (),
                            war=() if (kc or hh) else [r_ps1],
                            inc=ct if last else None)
                p.emit("scalar", lambda s: s.activation(sq1.ap, r_ps0.ap, AF.Square),
                       reads=[r_ps0], writes=[sq1], inc=cs)
                p.emit("scalar", lambda s: s.activation(sq2.ap, r_ps1.ap, AF.Square),
                       reads=[r_ps1], writes=[sq2], inc=cs)
                p.emit("vector", lambda v: v.tensor_add(sq1.ap, sq1.ap, sq2.ap),
                       reads=[sq1, sq2], writes=[sq1], inc=cv)
                p.emit("scalar", lambda s: s.activation(sq2.ap, sq1.ap, AF.Sqrt),
                       reads=[sq1], writes=[sq2], inc=cs)
                ob = omg[mc % 2]
                sl = slice(mc * TB, (mc + 1) * TB)
                p.emit("vector", lambda v, ob=ob, mc=mc, sl=sl:
                       v.scalar_tensor_tensor(ob.ap, sq2.ap,
                                              gcols[:, DC + mc:DC + mc + 1],
                                              r1s.ap[:, sl], ALU.mult, ALU.add),
                       reads=[sq2, g_b, r1s], writes=[ob], inc=cv)
                p.emit("sync", lambda s, ob=ob, mc=mc, t0=t0, t1=t1: s.dma_start(
                    out=outT[mc * 128:(mc + 1) * 128, t0:t1], in_=ob.ap),
                    reads=[ob], inc=d_o)

        p.emit("sync", None, extra_waits=[d_o])
        p.build(block)
    return nc


_NC_CACHE = {}


def kernel(L0_attn_x, L0_attn_A_log, L0_attn_Dp, L0_attn_W_xproj, L0_attn_b_xproj,
           L0_attn_W_out, L0_attn_b_out, L0_ffn_theta, L0_ffn_phi,
           L0_ffn_theta_noise, L0_ffn_phi_noise, L0_c, L0_wmod):
    B = L0_attn_x.shape[0]
    f32 = np.float32
    cc = np.ascontiguousarray

    A_log_r = cc(np.asarray(L0_attn_A_log, f32).reshape(DC, 128, N_ST)
                 .transpose(1, 0, 2).reshape(128, DC * N_ST))
    Dp_c = cc(np.asarray(L0_attn_Dp, f32).reshape(DC, 128).T)
    bxp_c = cc(np.asarray(L0_attn_b_xproj[:D], f32).reshape(DC, 128).T)
    bbc_c = cc(np.asarray(L0_attn_b_xproj[D:], f32).reshape(2 * N_ST, 1))
    bout_c = cc(np.asarray(L0_attn_b_out, f32).reshape(DC, 128).T)

    if "nc" not in _NC_CACHE:
        _NC_CACHE["nc"] = build_program()
    nc = _NC_CACHE["nc"]

    in_maps = []
    for b in range(B):
        in_maps.append(dict(
            xT=cc(np.asarray(L0_attn_x[b], f32).T),
            A_log_r=A_log_r,
            Dp_c=Dp_c,
            Wxp=cc(np.asarray(L0_attn_W_xproj, f32)),
            bxp_c=bxp_c,
            bbc_c=bbc_c,
            Wout=cc(np.asarray(L0_attn_W_out, f32)),
            bout_c=bout_c,
            theta=cc(np.asarray(L0_ffn_theta, f32)),
            phi=cc(np.asarray(L0_ffn_phi, f32)),
            theta_n=cc(np.asarray(L0_ffn_theta_noise, f32)),
            phi_n=cc(np.asarray(L0_ffn_phi_noise, f32)),
            c_col=cc(np.asarray(L0_c[b], f32).reshape(DC, 128).T),
            wmod=cc(np.asarray(L0_wmod, f32)),
        ))

    res = run_bass_kernel_spmd(nc, in_maps, core_ids=list(range(B)))
    out = np.empty((B, L, D), np.float32)
    for b in range(B):
        out[b] = res.results[b]["outT"].T
    return out


# version 09
# speedup vs baseline: 1.2140x; 1.1068x over previous
import numpy as np
import ml_dtypes
from contextlib import ExitStack

import concourse.bass as bass
import concourse.mybir as mybir
from concourse.bass_utils import run_bass_kernel_spmd

F32 = mybir.dt.float32
BF16 = mybir.dt.bfloat16
AF = mybir.ActivationFunctionType
ALU = mybir.AluOpType

ENGINES = ("sync", "scalar", "vector", "tensor", "gpsimd")

D = 1024
L = 4096
N_ST = 16
DC = 8
TB = 1024
NTB = L // TB
EPS = 1e-5



class Chan:

    def __init__(self, handle, dma=False):
        self.h = handle
        self.dma = dma
        self.count = 0
        self.amt = 16 if dma else 1


class Buf:

    def __init__(self, ap, name=""):
        self.ap = ap
        self.name = name
        self.last_write = None
        self.readers = []


class Prog:
    def __init__(self, nc):
        self.nc = nc
        self.steps = {e: [] for e in ENGINES}
        self.hwm = {e: {} for e in ENGINES}
        self.all_chans = []

    def chan(self, es, name, dma=False):
        h = es.enter_context(self.nc.semaphore(name))
        c = Chan(h, dma=dma)
        self.all_chans.append(c)
        return c

    def emit(self, engine, fn, reads=(), writes=(), extra_waits=(), inc=None,
             war=()):
        need = {}

        def add(cc):
            if cc is None:
                return
            ch, v = cc
            if v <= 0:
                return
            if need.get(ch, 0) < v:
                need[ch] = v

        for b in reads:
            add(b.last_write)
        for b in list(writes) + list(war):
            add(b.last_write)
            for r in b.readers:
                add(r)
        for w in extra_waits:
            add(w if isinstance(w, tuple) else (w, w.count))

        if inc is not None and inc.dma:
            add((inc, inc.count))
        hw = self.hwm[engine]
        waits = []
        for ch, v in need.items():
            if hw.get(ch, 0) < v:
                waits.append((ch, v))
                hw[ch] = v
        if inc is not None:
            inc.count += inc.amt
        self.steps[engine].append((fn, waits, inc))
        tag = (inc, inc.count) if inc is not None else None
        for b in writes:
            b.last_write = tag
            b.readers = []
        if tag is not None:
            for b in reads:
                b.readers.append(tag)

    def barrier(self):
        for e in ENGINES:
            for ch in self.all_chans:
                if ch.count > 0 and self.hwm[e].get(ch, 0) < ch.count:
                    self.steps[e].append((None, [(ch, ch.count)], None))
                    self.hwm[e][ch] = ch.count

    def build(self, block):
        def mk(engine):
            def run(eng):
                for fn, waits, inc in self.steps[engine]:
                    for ch, v in waits:
                        eng.wait_ge(ch.h, v)
                    if fn is not None:
                        ins = fn(eng)
                        if inc is not None:
                            assert ins is not None
                            ins.then_inc(inc.h, inc.amt)
                    elif inc is not None:
                        eng.sem_inc(inc.h, inc.amt)
            return run
        block.sync(mk("sync"))
        block.scalar(mk("scalar"))
        block.vector(mk("vector"))
        block.tensor(mk("tensor"))
        block.gpsimd(mk("gpsimd"))



def build_program():
    nc = bass.Bass("TRN2", target_bir_lowering=False)

    xT = nc.dram_tensor("xT", [D, L], F32, kind="ExternalInput").ap()
    A_log_r = nc.dram_tensor("A_log_r", [128, DC * N_ST], F32, kind="ExternalInput").ap()
    Dp_c = nc.dram_tensor("Dp_c", [128, DC], F32, kind="ExternalInput").ap()
    Wxp = nc.dram_tensor("Wxp", [D, D + 2 * N_ST], F32, kind="ExternalInput").ap()
    bxp_c = nc.dram_tensor("bxp_c", [128, DC], F32, kind="ExternalInput").ap()
    bbc_c = nc.dram_tensor("bbc_c", [2 * N_ST, 1], F32, kind="ExternalInput").ap()
    Wout = nc.dram_tensor("Wout", [D, D], F32, kind="ExternalInput").ap()
    bout_c = nc.dram_tensor("bout_c", [128, DC], F32, kind="ExternalInput").ap()
    theta = nc.dram_tensor("theta", [D, D], F32, kind="ExternalInput").ap()
    phi = nc.dram_tensor("phi", [D, D], F32, kind="ExternalInput").ap()
    theta_n = nc.dram_tensor("theta_n", [D, D], F32, kind="ExternalInput").ap()
    phi_n = nc.dram_tensor("phi_n", [D, D], F32, kind="ExternalInput").ap()
    c_col = nc.dram_tensor("c_col", [128, DC], F32, kind="ExternalInput").ap()
    wmod = nc.dram_tensor("wmod", [D, 6 * D], F32, kind="ExternalInput").ap()

    outT = nc.dram_tensor("outT", [D, L], F32, kind="ExternalOutput").ap()

    h1_scr = nc.dram_tensor("h1_scr", [DC, 128, L], BF16).ap()
    du_scr = nc.dram_tensor("du_scr", [DC, 128, 2 * L], BF16).ap()
    wo_scr = nc.dram_tensor("wo_scr", [D, D], BF16).ap()
    wc_scr = nc.dram_tensor("wc_scr", [D, D], BF16).ap()
    ws_scr = nc.dram_tensor("ws_scr", [D, D], BF16).ap()

    es = ExitStack()
    with es:
        p = Prog(nc)

        def sb(name, shape, dt):
            return es.enter_context(nc.sbuf_tensor(name, shape, dt))

        big1 = sb("big1", [128, 8 * L], BF16)
        big2 = sb("big2", [128, 8 * L], BF16)
        big4 = sb("big4", [128, 3 * L], BF16)
        wor = sb("wor", [128, 2 * DC * 128], BF16)
        wcr = sb("wcr", [128, 2 * DC * 128], BF16)
        wsr = sb("wsr", [128, 2 * DC * 128], BF16)
        bc_sb = sb("bc_sb", [2 * N_ST, L], BF16)
        bcstg = sb("bcstg", [1, L], BF16)
        cons = sb("cons", [128, 64], F32)
        a_sc = sb("a_sc", [128, 128], F32)
        mc_adj = sb("mc_adj", [128, 32], BF16)
        rows1 = sb("rows1", [2, D], BF16)
        rows2 = sb("rows2", [2, D], BF16)
        ab_a = sb("ab_a", [1, TB], BF16)
        ab_bo = sb("ab_bo", [2, TB], BF16)
        rowf = sb("rowf", [1, 3 * TB], F32)
        ones1 = sb("ones1", [128, 1], BF16)
        ones_r = sb("ones_r", [1, 128], BF16)
        sio = sb("sio", [128, DC], F32)
        siob = sb("siob", [128, DC], BF16)
        gcols = sb("gcols", [128, 4 * DC], F32)
        mcols_f = sb("mcols_f", [128, 48], F32)

        ps_all = es.enter_context(nc.psum_tensor("ps_all", [128, 4 * TB], F32))

        d_g = p.chan(es, "d_g", dma=True)
        d_s = p.chan(es, "d_s", dma=True)
        d_w = p.chan(es, "d_w", dma=True)
        d_o = p.chan(es, "d_o", dma=True)
        cv = p.chan(es, "cv")
        cs = p.chan(es, "cs")
        ct = p.chan(es, "ct")
        cg = p.chan(es, "cg")

        block = es.enter_context(nc.Block())

        asc_b = Buf(a_sc[:], "a_sc")
        p.emit("sync", lambda s: s.dma_start(out=a_sc[:], in_=A_log_r),
               writes=[asc_b], inc=d_s)
        p.emit("scalar", lambda s: s.activation(a_sc[:], a_sc[:], AF.Exp),
               reads=[asc_b], writes=[asc_b], inc=cs)
        p.emit("vector", lambda v: v.tensor_scalar_mul(a_sc[:], a_sc[:], -1.0),
               reads=[asc_b], writes=[asc_b], inc=cv)

        dp_b = Buf(cons[:, 0:DC], "dp")
        bxpc_b = Buf(cons[:, 8:8 + DC], "bxp")
        boutc_b = Buf(cons[:, 16:16 + DC], "bout")
        bbc_b = Buf(cons[0:2 * N_ST, 24:25], "bbc")
        p.emit("sync", lambda s: s.dma_start(out=cons[:, 0:DC], in_=Dp_c),
               writes=[dp_b], inc=d_s)
        p.emit("sync", lambda s: s.dma_start(out=cons[:, 8:8 + DC], in_=bxp_c),
               writes=[bxpc_b], inc=d_s)
        p.emit("sync", lambda s: s.dma_start(out=cons[:, 16:16 + DC], in_=bout_c),
               writes=[boutc_b], inc=d_s)
        p.emit("sync", lambda s: s.dma_start(out=cons[0:2 * N_ST, 24:25], in_=bbc_c),
               writes=[bbc_b], inc=d_s)

        sio_b = Buf(sio[:], "sio")
        siob_b = Buf(siob[:], "siob")
        p.emit("sync", lambda s: s.dma_start(out=sio[:], in_=c_col),
               writes=[sio_b], inc=d_s)
        p.emit("scalar", lambda s: s.activation(sio[:], sio[:], AF.Silu),
               reads=[sio_b], writes=[sio_b], inc=cs)
        p.emit("vector", lambda v: v.tensor_copy(siob[:], sio[:]),
               reads=[sio_b], writes=[siob_b], inc=cv)

        stg_a = Buf(big2[:, 0:8 * D], "stg_a")
        stg_b = Buf(big2[:, 8 * D:16 * D], "stg_b")
        stg_c = Buf(big2[:, 16 * D:24 * D], "stg_c")
        stg_d = Buf(big2[:, 24 * D:32 * D], "stg_d")

        def cast_weight_to_scr(w_in, w_out, stg):
            src = w_in.rearrange("(c p) j -> p c j", p=128)
            dst = w_out.rearrange("(c p) j -> p c j", p=128)
            sview = stg.ap.rearrange("p (c j) -> p c j", c=DC)
            p.emit("gpsimd", lambda g: g.dma_start(out=sview, in_=src),
                   writes=[stg], inc=d_g)
            p.emit("sync", lambda s: s.dma_start(out=dst, in_=sview),
                   reads=[stg], inc=d_w)

        cast_weight_to_scr(Wout, wo_scr, stg_a)

        def build_trig(w1, w2, w_out, stg1, stg2, is_cos):
            src1 = w1.rearrange("(c p) j -> p c j", p=128)
            src2 = w2.rearrange("(c p) j -> p c j", p=128)
            s1 = stg1.ap
            s2 = stg2.ap
            v1 = s1.rearrange("p (c j) -> p c j", c=DC)
            v2 = s2.rearrange("p (c j) -> p c j", c=DC)
            p.emit("gpsimd", lambda g: g.dma_start(out=v1, in_=src1),
                   writes=[stg1], inc=d_g)
            p.emit("gpsimd", lambda g: g.dma_start(out=v2, in_=src2),
                   writes=[stg2], inc=d_g)
            p.emit("vector", lambda v: v.tensor_add(s1, s1, s2),
                   reads=[stg1, stg2], writes=[stg1], inc=cv)
            p.emit("vector", lambda v: v.tensor_mul(s2, s1, s1),
                   reads=[stg1], writes=[stg2], inc=cv)
            if is_cos:
                p.emit("vector", lambda v: v.tensor_scalar(s2, s2, 1.0 / 24.0, -0.5,
                                                           ALU.mult, ALU.add),
                       reads=[stg2], writes=[stg2], inc=cv)
                p.emit("vector", lambda v: v.tensor_mul(s2, s2, s1),
                       reads=[stg1, stg2], writes=[stg2], inc=cv)
                p.emit("vector", lambda v: v.tensor_mul(s2, s2, s1),
                       reads=[stg1, stg2], writes=[stg2], inc=cv)
                p.emit("vector", lambda v: v.tensor_scalar_add(s1, s2, 1.0),
                       reads=[stg2], writes=[stg1], inc=cv)
            dst = w_out.rearrange("(c p) j -> p c j", p=128)
            sview = s1.rearrange("p (c j) -> p c j", c=DC)
            p.emit("sync", lambda s: s.dma_start(out=dst, in_=sview),
                   reads=[stg1], inc=d_w)

        build_trig(theta, theta_n, wc_scr, stg_a, stg_b, True)

        srcp = phi.rearrange("(c p) j -> p c j", p=128)
        srcpn = phi_n.rearrange("(c p) j -> p c j", p=128)
        sA, sB = stg_c, stg_d
        vA = sA.ap.rearrange("p (c j) -> p c j", c=DC)
        vB = sB.ap.rearrange("p (c j) -> p c j", c=DC)
        p.emit("gpsimd", lambda g: g.dma_start(out=vA, in_=srcp), writes=[sA], inc=d_g)
        p.emit("gpsimd", lambda g: g.dma_start(out=vB, in_=srcpn), writes=[sB], inc=d_g)
        p.emit("vector", lambda v: v.tensor_add(sA.ap, sA.ap, sB.ap),
               reads=[sA, sB], writes=[sA], inc=cv)
        p.emit("vector", lambda v: v.tensor_mul(sB.ap, sA.ap, sA.ap),
               reads=[sA], writes=[sB], inc=cv)
        p.emit("vector", lambda v: v.tensor_scalar(sB.ap, sB.ap, 1.0 / 120.0,
                                                   -1.0 / 6.0, ALU.mult, ALU.add),
               reads=[sB], writes=[sB], inc=cv)
        p.emit("vector", lambda v: v.tensor_mul(sB.ap, sB.ap, sA.ap),
               reads=[sA, sB], writes=[sB], inc=cv)
        p.emit("vector", lambda v: v.tensor_mul(sB.ap, sB.ap, sA.ap),
               reads=[sA, sB], writes=[sB], inc=cv)
        p.emit("vector", lambda v: v.tensor_mul(sB.ap, sB.ap, sA.ap),
               reads=[sA, sB], writes=[sB], inc=cv)
        p.emit("vector", lambda v: v.tensor_add(sB.ap, sB.ap, sA.ap),
               reads=[sA, sB], writes=[sB], inc=cv)
        dstws = ws_scr.rearrange("(c p) j -> p c j", p=128)
        p.emit("sync", lambda s: s.dma_start(out=dstws, in_=vB), reads=[sB], inc=d_w)

        wm1 = Buf(big1[:, 0:6 * D], "wm1")
        wm2 = Buf(big1[:, 6 * D:12 * D], "wm2")
        mcol_ps = Buf(ps_all[:, 0:48], "mcol_ps")
        mcols_b = Buf(mcols_f[:], "mcols")
        p.emit("vector", lambda v: v.memset(mcols_f[:], 0.0),
               writes=[mcols_b], inc=cv)
        for kc in range(DC):
            wmb = (wm1, wm2)[kc % 2]
            src = wmod[kc * 128:(kc + 1) * 128, :]
            p.emit("gpsimd", lambda g, wmb=wmb, src=src:
                   g.dma_start(out=wmb.ap, in_=src), writes=[wmb], inc=d_g)
            for j in range(48):
                grp, dcj = j // DC, j % DC
                last = (j == 47)
                p.emit("tensor", lambda t, wmb=wmb, kc=kc, j=j, grp=grp, dcj=dcj:
                       t.matmul(mcol_ps.ap[:, j:j + 1],
                                wmb.ap[:, grp * D + dcj * 128:grp * D + (dcj + 1) * 128],
                                siob[:, kc:kc + 1],
                                start=True, stop=True),
                       reads=[wmb, siob_b], writes=[mcol_ps] if last else (),
                       war=() if j else [mcol_ps],
                       inc=ct if last else None)
            p.emit("vector", lambda v: v.tensor_add(mcols_f[:], mcols_f[:],
                                                    mcol_ps.ap),
                   reads=[mcol_ps], writes=[mcols_b], inc=cv)

        mc_b = Buf(mc_adj[:], "mc_adj")
        p.emit("vector", lambda v: v.tensor_scalar_add(
            mc_adj[:, 0:16:2], mcols_f[:, 8:16], 1.0),
            reads=[mcols_b], writes=[mc_b], inc=cv)
        p.emit("vector", lambda v: v.tensor_copy(mc_adj[:, 1:16:2], mcols_f[:, 0:8]),
               reads=[mcols_b], writes=[mc_b], inc=cv)
        p.emit("vector", lambda v: v.tensor_scalar_add(
            mc_adj[:, 16:32:2], mcols_f[:, 32:40], 1.0),
            reads=[mcols_b], writes=[mc_b], inc=cv)
        p.emit("vector", lambda v: v.tensor_copy(mc_adj[:, 17:32:2], mcols_f[:, 24:32]),
               reads=[mcols_b], writes=[mc_b], inc=cv)

        rows1_b = Buf(rows1[:], "rows1")
        rows2_b = Buf(rows2[:], "rows2")
        for ln_i, (rt, rb_) in enumerate(((rows1, rows1_b), (rows2, rows2_b))):
            for dc in range(DC):
                for r in range(2):
                    col = ln_i * 16 + 2 * dc + r
                    p.emit("sync", lambda s, rt=rt, r=r, dc=dc, col=col: s.dma_start(
                        out=rt[r:r + 1, dc * 128:(dc + 1) * 128],
                        in_=mc_adj[:, col:col + 1],
                        allow_slow_non_contiguous=True),
                        reads=[mc_b], writes=[rb_], inc=d_s)

        g_b = Buf(gcols[:], "gcols")
        for gi, base in enumerate((16, 40)):
            dst = gcols[:, gi * DC:(gi + 1) * DC]
            tmp = gcols[:, (2 + gi) * DC:(3 + gi) * DC]
            src = mcols_f[:, base:base + DC]
            p.emit("vector", lambda v, tmp=tmp, src=src:
                   v.tensor_scalar(tmp, src, 0.0, 2.0, ALU.is_ge, ALU.mult),
                   reads=[mcols_b], writes=[g_b], inc=cv)
            p.emit("vector", lambda v, tmp=tmp:
                   v.tensor_scalar_add(tmp, tmp, -1.0),
                   reads=[g_b], writes=[g_b], inc=cv)
            p.emit("scalar", lambda s, dst=dst, src=src:
                   s.activation(dst, src, AF.Abs),
                   reads=[mcols_b], writes=[g_b], inc=cs)
            p.emit("vector", lambda v, dst=dst:
                   v.tensor_scalar_max(dst, dst, 0.25),
                   reads=[g_b], writes=[g_b], inc=cv)
            p.emit("vector", lambda v, dst=dst, tmp=tmp:
                   v.tensor_mul(dst, dst, tmp),
                   reads=[g_b], writes=[g_b], inc=cv)

        ones_b = Buf(ones1[:], "ones1")
        p.emit("vector", lambda v: v.memset(ones1[:], 1.0), writes=[ones_b], inc=cv)
        onesr_b = Buf(ones_r[:], "ones_r")
        p.emit("vector", lambda v: v.memset(ones_r[:], 1.0), writes=[onesr_b], inc=cv)
        aba_b = Buf(ab_a[:], "ab_a")
        abbo_b = Buf(ab_bo[:], "ab_bo")
        p.emit("vector", lambda v: v.memset(ab_bo[0:2, :], 1.0),
               writes=[abbo_b], inc=cv)

        p.barrier()
        wx_b = Buf(big1[:, 0:DC * 1056], "wx")
        wxv = big1[:, 0:DC * 1056].rearrange("p (c j) -> p c j", c=DC)
        p.emit("gpsimd", lambda g: g.dma_start(
            out=wxv, in_=Wxp.rearrange("(c p) j -> p c j", p=128)),
            writes=[wx_b], inc=d_g)

        WX_END = DC * 1056
        xs_p = [Buf(big2[:, 0:8 * TB], "x_bf0"),
                Buf(big1[:, WX_END:WX_END + 8 * TB], "x_bf1")]
        h1_p = [Buf(big2[:, 8 * TB:16 * TB], "h1_0"),
                Buf(big1[:, WX_END + 8 * TB:WX_END + 16 * TB], "h1_1")]
        dls = Buf(big2[:, 16 * TB:24 * TB], "dl")
        uss = Buf(big2[:, 24 * TB:32 * TB], "u")
        xsq = Buf(big4[:, 0:8 * TB], "xsq")
        esb = Buf(big4[:, 8 * TB:9 * TB], "esb")
        r_ps0 = Buf(ps_all[:, 0:TB], "ps0")
        r_ps1 = Buf(ps_all[:, TB:2 * TB], "ps1")
        r_ps2 = Buf(ps_all[:, 2 * TB:3 * TB], "ps2")
        r_ps3 = Buf(ps_all[:, 3 * TB:4 * TB], "ps3")
        rowf_b = Buf(rowf[:], "rowf")
        bc_b = Buf(bc_sb[:], "bc")

        def ln_stats_and_coeffs(src_b):
            p.emit("vector", lambda v: v.tensor_mul(xsq.ap, src_b.ap, src_b.ap),
                   reads=[src_b], writes=[xsq], inc=cv)
            for dc in range(DC):
                for hh in range(2):
                    last = (dc == DC - 1 and hh == 1)
                    p.emit("tensor", lambda t, dc=dc, hh=hh: t.matmul(
                        r_ps0.ap[0:1, hh * 512:(hh + 1) * 512], ones1[:, 0:1],
                        src_b.ap[:, dc * TB + hh * 512:dc * TB + (hh + 1) * 512],
                        start=(dc == 0), stop=(dc == DC - 1)),
                        reads=[src_b, ones_b], writes=[r_ps0] if last else (),
                        war=() if (dc or hh) else [r_ps0],
                        inc=ct if last else None)
            for dc in range(DC):
                for hh in range(2):
                    last = (dc == DC - 1 and hh == 1)
                    p.emit("tensor", lambda t, dc=dc, hh=hh: t.matmul(
                        r_ps1.ap[0:1, hh * 512:(hh + 1) * 512], ones1[:, 0:1],
                        xsq.ap[:, dc * TB + hh * 512:dc * TB + (hh + 1) * 512],
                        start=(dc == 0), stop=(dc == DC - 1)),
                        reads=[xsq, ones_b], writes=[r_ps1] if last else (),
                        war=() if (dc or hh) else [r_ps1],
                        inc=ct if last else None)
            p.emit("vector", lambda v: v.tensor_scalar_mul(rowf[0:1, 0:TB],
                                                           r_ps0.ap[0:1, :], 1.0 / D),
                   reads=[r_ps0], writes=[rowf_b], inc=cv)
            p.emit("scalar", lambda s: s.activation(rowf[0:1, TB:2 * TB], rowf[0:1, 0:TB],
                                                    AF.Square),
                   reads=[rowf_b], writes=[rowf_b], inc=cs)
            p.emit("vector", lambda v: v.tensor_scalar(rowf[0:1, 2 * TB:3 * TB],
                                                       r_ps1.ap[0:1, :], 1.0 / D,
                                                       EPS, ALU.mult, ALU.add),
                   reads=[r_ps1], writes=[rowf_b], inc=cv)
            p.emit("vector", lambda v: v.tensor_sub(rowf[0:1, 2 * TB:3 * TB], rowf[0:1, 2 * TB:3 * TB],
                                                    rowf[0:1, TB:2 * TB]),
                   reads=[rowf_b], writes=[rowf_b], inc=cv)
            p.emit("scalar", lambda s: s.activation(rowf[0:1, 2 * TB:3 * TB], rowf[0:1, 2 * TB:3 * TB],
                                                    AF.Ln),
                   reads=[rowf_b], writes=[rowf_b], inc=cs)
            p.emit("vector", lambda v: v.tensor_scalar_mul(rowf[0:1, 2 * TB:3 * TB],
                                                           rowf[0:1, 2 * TB:3 * TB], -0.5),
                   reads=[rowf_b], writes=[rowf_b], inc=cv)
            p.emit("scalar", lambda s: s.activation(rowf[0:1, 2 * TB:3 * TB], rowf[0:1, 2 * TB:3 * TB],
                                                    AF.Exp),
                   reads=[rowf_b], writes=[rowf_b], inc=cs)
            p.emit("vector", lambda v: v.tensor_copy(ab_a[0:1, :], rowf[0:1, 2 * TB:3 * TB]),
                   reads=[rowf_b], writes=[aba_b], inc=cv)
            p.emit("vector", lambda v: v.scalar_tensor_tensor(
                ab_bo[0:1, :], rowf[0:1, 0:TB], -1.0, rowf[0:1, 2 * TB:3 * TB],
                ALU.mult, ALU.mult),
                reads=[rowf_b], writes=[abbo_b], inc=cv)

        def ln_apply(dst_b, src_b, rows_t, rows_tb):
            for dc in range(DC):
                sl128 = slice(dc * 128, (dc + 1) * 128)
                for hh in range(2):
                    h5 = slice(hh * 512, (hh + 1) * 512)
                    p.emit("tensor", lambda t, sl128=sl128, rows_t=rows_t, h5=h5:
                           t.matmul(r_ps0.ap[:, h5], rows_t[0:1, sl128],
                                    ab_a[0:1, h5], start=True, stop=True),
                           reads=[rows_tb, aba_b], writes=[r_ps0], inc=ct)
                    p.emit("tensor", lambda t, sl128=sl128, rows_t=rows_t, h5=h5:
                           t.matmul(r_ps1.ap[:, h5], rows_t[0:2, sl128],
                                    ab_bo[0:2, h5], start=True, stop=True),
                           reads=[rows_tb, abbo_b], writes=[r_ps1], inc=ct)
                sl = slice(dc * TB, (dc + 1) * TB)
                p.emit("vector", lambda v, sl=sl: v.tensor_mul(
                    dst_b.ap[:, sl], src_b.ap[:, sl], r_ps0.ap[:, :]),
                    reads=[src_b, r_ps0], writes=[dst_b], inc=cv)
                p.emit("vector", lambda v, sl=sl: v.tensor_add(
                    dst_b.ap[:, sl], dst_b.ap[:, sl], r_ps1.ap[:, :]),
                    reads=[r_ps1], writes=[dst_b], inc=cv)

        for tb in range(NTB):
            t0, t1 = tb * TB, (tb + 1) * TB
            xs, h1s = xs_p[tb % 2], h1_p[tb % 2]
            p.emit("gpsimd", lambda g, xs=xs, t0=t0, t1=t1: g.dma_start(
                out=xs.ap.rearrange("p (c j) -> p c j", c=DC),
                in_=xT.rearrange("(c p) t -> p c t", p=128)[:, :, t0:t1]),
                writes=[xs], inc=d_g)
            ln_stats_and_coeffs(xs)
            ln_apply(h1s, xs, rows1, rows1_b)
            for mc in range(9):
                w = 128 if mc < 8 else 2 * N_ST
                pj = (r_ps2, r_ps3)[mc % 2]
                for kc in range(DC):
                    for hh in range(2):
                        last = (kc == DC - 1 and hh == 1)
                        p.emit("tensor",
                               lambda t, mc=mc, kc=kc, w=w, pj=pj, hh=hh, h1s=h1s:
                               t.matmul(
                            pj.ap[0:w, hh * 512:(hh + 1) * 512],
                            big1[:, kc * 1056 + mc * 128:kc * 1056 + mc * 128 + w],
                            h1s.ap[:, kc * TB + hh * 512:kc * TB + (hh + 1) * 512],
                            start=(kc == 0), stop=(kc == DC - 1)),
                            reads=[wx_b, h1s], writes=[pj] if last else (),
                            war=() if (kc or hh) else [pj],
                            inc=ct if last else None)
                if mc < 8:
                    sl = slice(mc * TB, (mc + 1) * TB)
                    p.emit("scalar", lambda s, pj=pj, mc=mc: s.activation(
                        esb.ap, pj.ap, AF.Exp, bias=cons[:, 8 + mc:9 + mc]),
                        reads=[pj, bxpc_b], writes=[esb], inc=cs)
                    p.emit("scalar", lambda s, sl=sl: s.activation(
                        dls.ap[:, sl], esb.ap, AF.Ln, bias=1.0),
                        reads=[esb], writes=[dls], inc=cs)
                    p.emit("vector", lambda v, sl=sl, h1s=h1s: v.tensor_mul(
                        uss.ap[:, sl], dls.ap[:, sl], h1s.ap[:, sl]),
                        reads=[dls, h1s], writes=[uss], inc=cv)
                else:
                    p.emit("vector", lambda v, pj=pj, t0=t0, t1=t1:
                           v.tensor_single_scalar(
                        bc_sb[:, t0:t1], pj.ap[0:2 * N_ST, :],
                        cons[0:2 * N_ST, 24:25], ALU.add),
                        reads=[pj, bbc_b], writes=[bc_b], inc=cv)
            for (buf, scr, off) in ((h1s, h1_scr, 0), (dls, du_scr, 0),
                                    (uss, du_scr, L)):
                p.emit("sync", lambda s, buf=buf, scr=scr, t0=t0, t1=t1, off=off:
                       s.dma_start(
                    out=scr[:, :, off + t0:off + t1].transpose([1, 0, 2]),
                    in_=buf.ap.rearrange("p (c j) -> p c j", c=DC)),
                    reads=[buf], inc=d_w)

        p.barrier()
        y_dc = [Buf(big1[:, dc * L:(dc + 1) * L], f"y{dc}") for dc in range(DC)]
        brep = [Buf(big2[:, i * L:(i + 1) * L], f"brep{i}") for i in range(2)]
        crep = [Buf(big2[:, (2 + i) * L:(3 + i) * L], f"crep{i}") for i in range(2)]
        dustr = [Buf(big2[:, (4 + 2 * i) * L:(6 + 2 * i) * L], f"dustr{i}")
                 for i in range(2)]
        zg_t = Buf(wor[:, 0:2048], "zg")
        abar_t = Buf(big4[:, 0:L], "abar")
        abar = [abar_t, abar_t]
        bx_t = Buf(big4[:, L:2 * L], "bx")
        h_t = Buf(big4[:, 2 * L:3 * L], "h")
        brep_ps = Buf(ps_all[:, :], "brep_ps")

        for dc in range(DC):
            p.emit("vector", lambda v, dc=dc: v.memset(y_dc[dc].ap, 0.0),
                   writes=[y_dc[dc]], inc=cv)

        bcstg_b = [Buf(bcstg[0:1, 0:L], "bcstg"), Buf(bcstg[0:1, 0:L], "bcstg")]
        bcstg_b[1] = bcstg_b[0]
        for n in range(N_ST):
            bb, cb_ = brep[n % 2], crep[n % 2]
            for si, (rb, row) in enumerate(((bb, n), (cb_, N_ST + n))):
                stg = bcstg_b[si]
                p.emit("sync", lambda s, stg=stg, row=row: s.dma_start(
                    out=stg.ap, in_=bc_sb[row:row + 1, :]),
                    reads=[bc_b], writes=[stg], inc=d_s)
                for q in range(8):
                    last = (q == 7)
                    p.emit("tensor", lambda t, stg=stg, q=q: t.matmul(
                        brep_ps.ap[:, q * 512:(q + 1) * 512],
                        ones_r[0:1, :],
                        stg.ap[:, q * 512:(q + 1) * 512],
                        start=True, stop=True),
                        reads=[stg, onesr_b], writes=[brep_ps] if last else (),
                        war=() if q else [brep_ps],
                        inc=ct if last else None)
                p.emit("scalar", lambda s, rb=rb: s.activation(rb.ap, brep_ps.ap,
                                                               AF.Copy),
                       reads=[brep_ps], writes=[rb], inc=cs)
            for dc in range(DC):
                du = dustr[dc % 2]
                p.emit("sync", lambda s, du=du, dc=dc: s.dma_start(
                    out=du.ap, in_=du_scr[dc]), writes=[du], inc=d_s)
                ab = abar[dc % 2]
                p.emit("scalar", lambda s, ab=ab, du=du, dc=dc, n=n: s.activation(
                    ab.ap, du.ap[:, 0:L], AF.Exp,
                    scale=a_sc[:, dc * N_ST + n:dc * N_ST + n + 1]),
                    reads=[du, asc_b], writes=[ab], inc=cs)
                p.emit("vector", lambda v, du=du, bb=bb: v.tensor_mul(
                    bx_t.ap, du.ap[:, L:2 * L], bb.ap),
                    reads=[du, bb], writes=[bx_t], inc=cv)
                p.emit("vector", lambda v, ab=ab: v.tensor_tensor_scan(
                    h_t.ap, ab.ap, bx_t.ap, 0.0, ALU.mult, ALU.add),
                    reads=[ab, bx_t], writes=[h_t], inc=cv)
                p.emit("vector", lambda v, cb_=cb_: v.tensor_mul(
                    bx_t.ap, h_t.ap, cb_.ap),
                    reads=[h_t, cb_], writes=[bx_t], inc=cv)
                p.emit("vector", lambda v, dc=dc: v.tensor_add(
                    y_dc[dc].ap, y_dc[dc].ap, bx_t.ap),
                    reads=[bx_t], writes=[y_dc[dc]], inc=cv)

        for dc in range(DC):
            du = dustr[dc % 2]
            hv = Buf(du.ap[:, 0:L], f"h1v{dc}")
            hv.last_write = du.last_write
            hv.readers = list(du.readers)
            p.emit("sync", lambda s, du=du, dc=dc: s.dma_start(
                out=du.ap[:, 0:L], in_=h1_scr[dc]),
                reads=[], writes=[du], inc=d_s)
            p.emit("vector", lambda v, du=du, dc=dc: v.scalar_tensor_tensor(
                y_dc[dc].ap, du.ap[:, 0:L], cons[:, dc:dc + 1], y_dc[dc].ap,
                ALU.mult, ALU.add),
                reads=[du, dp_b], writes=[y_dc[dc]], inc=cv)

        p.barrier()
        r1s = Buf(big2[:, 0:8 * TB], "r1")
        h2s = Buf(big2[:, 8 * TB:16 * TB], "h2")
        wcf = Buf(big2[:, 16 * TB:24 * TB], "wc_full")
        wsf = Buf(big2[:, 24 * TB:32 * TB], "ws_full")
        xb2 = Buf(big4[:, 0:8 * TB], "xb2")
        sqr = [Buf(big4[:, 8 * TB + i * TB:8 * TB + (i + 1) * TB], f"sqr{i}")
               for i in range(2)]
        sq1 = Buf(wcr[:].bitcast(F32), "sq1")
        sq2 = Buf(wsr[:].bitcast(F32), "sq2")
        omg = Buf(big4[:, 10 * TB:12 * TB].bitcast(F32), "omg")
        worb = [Buf(wor[:, i * DC * 128:(i + 1) * DC * 128], f"wor{i}") for i in range(2)]

        for (scr, buf) in ((wc_scr, wcf), (ws_scr, wsf)):
            p.emit("sync", lambda s, scr=scr, buf=buf: s.dma_start(
                out=buf.ap.rearrange("p (c j) -> p c j", c=DC),
                in_=scr.rearrange("(c p) j -> p c j", p=128)),
                writes=[buf], inc=d_s)

        def load_wring(scr, rb, mc):
            src = scr.rearrange("(c p) j -> p c j", p=128)[:, :, mc * 128:(mc + 1) * 128]
            p.emit("sync", lambda s: s.dma_start(
                out=rb.ap.rearrange("p (c j) -> p c j", c=DC), in_=src),
                writes=[rb], inc=d_s)

        for tb in range(NTB):
            t0, t1 = tb * TB, (tb + 1) * TB
            p.emit("gpsimd", lambda g, t0=t0, t1=t1: g.dma_start(
                out=xb2.ap.rearrange("p (c j) -> p c j", c=DC),
                in_=xT.rearrange("(c p) t -> p c t", p=128)[:, :, t0:t1]),
                writes=[xb2], inc=d_g)
            for mc in range(DC):
                rb = worb[mc % 2]
                load_wring(wo_scr, rb, mc)
                pj = (r_ps0, r_ps1)[mc % 2]
                for kc in range(DC):
                    for hh in range(2):
                        last = (kc == DC - 1 and hh == 1)
                        p.emit("tensor",
                               lambda t, rb=rb, kc=kc, pj=pj, t0=t0, hh=hh:
                               t.matmul(pj.ap[:, hh * 512:(hh + 1) * 512],
                                        rb.ap[:, kc * 128:(kc + 1) * 128],
                                        big1[:, kc * L + t0 + hh * 512:
                                             kc * L + t0 + (hh + 1) * 512],
                                        start=(kc == 0), stop=(kc == DC - 1)),
                               reads=[rb, y_dc[kc]], writes=[pj] if last else (),
                               war=() if (kc or hh) else [pj],
                               inc=ct if last else None)
                sl = slice(mc * TB, (mc + 1) * TB)
                p.emit("vector", lambda v, pj=pj, mc=mc, sl=sl: v.tensor_scalar(
                    h2s.ap[:, sl], pj.ap, cons[:, 16 + mc:17 + mc],
                    gcols[:, mc:mc + 1], ALU.add, ALU.mult),
                    reads=[pj, boutc_b, g_b], writes=[h2s], inc=cv)
                p.emit("vector", lambda v, sl=sl: v.tensor_add(
                    r1s.ap[:, sl], h2s.ap[:, sl], xb2.ap[:, sl]),
                    reads=[h2s, xb2], writes=[r1s], inc=cv)
            for dc in range(DC):
                for hh in range(2):
                    last = (dc == DC - 1 and hh == 1)
                    p.emit("tensor", lambda t, dc=dc, hh=hh: t.matmul(
                        r_ps2.ap[0:1, hh * 512:(hh + 1) * 512], ones1[:, 0:1],
                        r1s.ap[:, dc * TB + hh * 512:dc * TB + (hh + 1) * 512],
                        start=(dc == 0), stop=(dc == DC - 1)),
                        reads=[r1s, ones_b], writes=[r_ps2] if last else (),
                        war=() if (dc or hh) else [r_ps2],
                        inc=ct if last else None)
            for dc in range(DC):
                sb_ = sqr[dc % 2]
                sl = slice(dc * TB, (dc + 1) * TB)
                p.emit("vector", lambda v, sb_=sb_, sl=sl: v.tensor_mul(
                    sb_.ap, r1s.ap[:, sl], r1s.ap[:, sl]),
                    reads=[r1s], writes=[sb_], inc=cv)
                for hh in range(2):
                    last = (dc == DC - 1 and hh == 1)
                    p.emit("tensor", lambda t, sb_=sb_, dc=dc, hh=hh: t.matmul(
                        r_ps3.ap[0:1, hh * 512:(hh + 1) * 512], ones1[:, 0:1],
                        sb_.ap[:, hh * 512:(hh + 1) * 512],
                        start=(dc == 0), stop=(dc == DC - 1)),
                        reads=[sb_, ones_b], writes=[r_ps3] if last else (),
                        war=() if (dc or hh) else [r_ps3],
                        inc=ct if hh == 1 else None)
            p.emit("vector", lambda v: v.tensor_scalar_mul(rowf[0:1, 0:TB],
                                                           r_ps2.ap[0:1, :], 1.0 / D),
                   reads=[r_ps2], writes=[rowf_b], inc=cv)
            p.emit("scalar", lambda s: s.activation(rowf[0:1, TB:2 * TB], rowf[0:1, 0:TB],
                                                    AF.Square),
                   reads=[rowf_b], writes=[rowf_b], inc=cs)
            p.emit("vector", lambda v: v.tensor_scalar(rowf[0:1, 2 * TB:3 * TB],
                                                       r_ps3.ap[0:1, :], 1.0 / D,
                                                       EPS, ALU.mult, ALU.add),
                   reads=[r_ps3], writes=[rowf_b], inc=cv)
            p.emit("vector", lambda v: v.tensor_sub(rowf[0:1, 2 * TB:3 * TB], rowf[0:1, 2 * TB:3 * TB],
                                                    rowf[0:1, TB:2 * TB]),
                   reads=[rowf_b], writes=[rowf_b], inc=cv)
            p.emit("scalar", lambda s: s.activation(rowf[0:1, 2 * TB:3 * TB], rowf[0:1, 2 * TB:3 * TB],
                                                    AF.Sqrt),
                   reads=[rowf_b], writes=[rowf_b], inc=cs)
            p.emit("vector", lambda v: v.reciprocal(rowf[0:1, 2 * TB:3 * TB], rowf[0:1, 2 * TB:3 * TB]),
                   reads=[rowf_b], writes=[rowf_b], inc=cv)
            p.emit("vector", lambda v: v.tensor_copy(ab_a[0:1, :], rowf[0:1, 2 * TB:3 * TB]),
                   reads=[rowf_b], writes=[aba_b], inc=cv)
            p.emit("vector", lambda v: v.scalar_tensor_tensor(
                ab_bo[0:1, :], rowf[0:1, 0:TB], -1.0, rowf[0:1, 2 * TB:3 * TB],
                ALU.mult, ALU.mult),
                reads=[rowf_b], writes=[abbo_b], inc=cv)
            ln_apply(h2s, r1s, rows2, rows2_b)
            for mc in range(DC):
                for kc in range(DC):
                    for hh in range(2):
                        last = (kc == DC - 1 and hh == 1)
                        p.emit("tensor", lambda t, mc=mc, kc=kc, hh=hh: t.matmul(
                            r_ps0.ap[:, hh * 512:(hh + 1) * 512],
                            wcf.ap[:, kc * TB + mc * 128:kc * TB + mc * 128 + 128],
                            h2s.ap[:, kc * TB + hh * 512:kc * TB + (hh + 1) * 512],
                            start=(kc == 0), stop=(kc == DC - 1)),
                            reads=[wcf, h2s], writes=[r_ps0] if last else (),
                            war=() if (kc or hh) else [r_ps0],
                            inc=ct if last else None)
                for kc in range(DC):
                    for hh in range(2):
                        last = (kc == DC - 1 and hh == 1)
                        p.emit("tensor", lambda t, mc=mc, kc=kc, hh=hh: t.matmul(
                            r_ps1.ap[:, hh * 512:(hh + 1) * 512],
                            wsf.ap[:, kc * TB + mc * 128:kc * TB + mc * 128 + 128],
                            h2s.ap[:, kc * TB + hh * 512:kc * TB + (hh + 1) * 512],
                            start=(kc == 0), stop=(kc == DC - 1)),
                            reads=[wsf, h2s], writes=[r_ps1] if last else (),
                            war=() if (kc or hh) else [r_ps1],
                            inc=ct if last else None)
                p.emit("scalar", lambda s: s.activation(sq1.ap, r_ps0.ap, AF.Square),
                       reads=[r_ps0], writes=[sq1], inc=cs)
                p.emit("scalar", lambda s: s.activation(sq2.ap, r_ps1.ap, AF.Square),
                       reads=[r_ps1], writes=[sq2], inc=cs)
                p.emit("vector", lambda v: v.tensor_add(sq1.ap, sq1.ap, sq2.ap),
                       reads=[sq1, sq2], writes=[sq1], inc=cv)
                p.emit("scalar", lambda s: s.activation(sq2.ap, sq1.ap, AF.Sqrt),
                       reads=[sq1], writes=[sq2], inc=cs)
                sl = slice(mc * TB, (mc + 1) * TB)
                p.emit("vector", lambda v, mc=mc, sl=sl:
                       v.scalar_tensor_tensor(omg.ap, sq2.ap,
                                              gcols[:, DC + mc:DC + mc + 1],
                                              r1s.ap[:, sl], ALU.mult, ALU.add),
                       reads=[sq2, g_b, r1s], writes=[omg], inc=cv)
                p.emit("sync", lambda s, mc=mc, t0=t0, t1=t1: s.dma_start(
                    out=outT[mc * 128:(mc + 1) * 128, t0:t1], in_=omg.ap),
                    reads=[omg], inc=d_o)

        p.emit("sync", None, extra_waits=[d_o])
        p.build(block)
    return nc


_NC_CACHE = {}


def kernel(L0_attn_x, L0_attn_A_log, L0_attn_Dp, L0_attn_W_xproj, L0_attn_b_xproj,
           L0_attn_W_out, L0_attn_b_out, L0_ffn_theta, L0_ffn_phi,
           L0_ffn_theta_noise, L0_ffn_phi_noise, L0_c, L0_wmod):
    B = L0_attn_x.shape[0]
    f32 = np.float32
    cc = np.ascontiguousarray

    A_log_r = cc(np.asarray(L0_attn_A_log, f32).reshape(DC, 128, N_ST)
                 .transpose(1, 0, 2).reshape(128, DC * N_ST))
    Dp_c = cc(np.asarray(L0_attn_Dp, f32).reshape(DC, 128).T)
    bxp_c = cc(np.asarray(L0_attn_b_xproj[:D], f32).reshape(DC, 128).T)
    bbc_c = cc(np.asarray(L0_attn_b_xproj[D:], f32).reshape(2 * N_ST, 1))
    bout_c = cc(np.asarray(L0_attn_b_out, f32).reshape(DC, 128).T)

    if "nc" not in _NC_CACHE:
        _NC_CACHE["nc"] = build_program()
    nc = _NC_CACHE["nc"]

    in_maps = []
    for b in range(B):
        in_maps.append(dict(
            xT=cc(np.asarray(L0_attn_x[b], f32).T),
            A_log_r=A_log_r,
            Dp_c=Dp_c,
            Wxp=cc(np.asarray(L0_attn_W_xproj, f32)),
            bxp_c=bxp_c,
            bbc_c=bbc_c,
            Wout=cc(np.asarray(L0_attn_W_out, f32)),
            bout_c=bout_c,
            theta=cc(np.asarray(L0_ffn_theta, f32)),
            phi=cc(np.asarray(L0_ffn_phi, f32)),
            theta_n=cc(np.asarray(L0_ffn_theta_noise, f32)),
            phi_n=cc(np.asarray(L0_ffn_phi_noise, f32)),
            c_col=cc(np.asarray(L0_c[b], f32).reshape(DC, 128).T),
            wmod=cc(np.asarray(L0_wmod, f32)),
        ))

    res = run_bass_kernel_spmd(nc, in_maps, core_ids=list(range(B)))
    out = np.empty((B, L, D), np.float32)
    for b in range(B):
        out[b] = res.results[b]["outT"].T
    return out
